# version 06 (final)
# baseline (speedup 1.0000x reference)
import numpy as np
import ml_dtypes
from contextlib import ExitStack

import concourse.bass as bass
import concourse.mybir as mybir
from concourse.bass_utils import run_bass_kernel_spmd

F32 = mybir.dt.float32
BF16 = mybir.dt.bfloat16
AF = mybir.ActivationFunctionType
ALU = mybir.AluOpType
AX = mybir.AxisListType
BFNP = ml_dtypes.bfloat16

D = 1024
NH = 16
NE = 16
FF = 4096
TOKS = 4096
WAVE = 512
NWAVES = TOKS // WAVE
TPW = WAVE // 128
NCB = D // 128
NDC = D // 128
NFT = FF // 128
EPS = 1e-5
BIG = 1.0e38


class Ctr:
    __slots__ = ("h", "n", "step")

    def __init__(self, h, step=1):
        self.h = h
        self.n = 0
        self.step = step

    def bump(self):
        self.n += self.step
        return self.n


class Prog:
    ENGS = ("tensor", "vector", "scalar", "sync", "gpsimd")

    def __init__(self):
        self.streams = {k: [] for k in self.ENGS}

    def wait(self, eng, ctr, val):
        if val is None or val <= 0:
            return
        h = ctr.h
        self.streams[eng].append(lambda E, h=h, v=val: E.wait_ge(h, v))

    def op(self, eng, build, inc=None):
        if inc is not None:
            tgt = inc.bump()
            h, s = inc.h, inc.step

            def fn(E, build=build, h=h, s=s):
                build(E).then_inc(h, s)
        else:
            tgt = None

            def fn(E, build=build):
                build(E)
        self.streams[eng].append(fn)
        return tgt


def build_program(n_waves=NWAVES, n_experts=NE, debug=False, diag_serial_w1=False):
    nc = bass.Bass("TRN2", target_bir_lowering=False)
    dt = nc.dram_tensor

    x = dt("x", [TOKS, D], F32, kind="ExternalInput").ap()
    wq = dt("wq", [D, D], F32, kind="ExternalInput").ap()
    wv = dt("wv", [D, D], F32, kind="ExternalInput").ap()
    wg = dt("wg", [D, D], F32, kind="ExternalInput").ap()
    wo = dt("wo", [D, D], F32, kind="ExternalInput").ap()
    beta = dt("beta", [NH, 1], F32, kind="ExternalInput").ap()
    gam = dt("gam", [D, 1], F32, kind="ExternalInput").ap()
    bet = dt("bet", [D, 1], F32, kind="ExternalInput").ap()
    keys = dt("keys", [NE, D], F32, kind="ExternalInput").ap()
    rwt = dt("rwt", [NE, D], F32, kind="ExternalInput").ap()
    rb = dt("rb", [NE, 1], F32, kind="ExternalInput").ap()
    b1 = dt("b1", [NE, FF], F32, kind="ExternalInput").ap()
    b2 = dt("b2", [NE, D], F32, kind="ExternalInput").ap()
    w1 = dt("w1", [NE, FF, D], F32, kind="ExternalInput").ap()
    w2 = dt("w2", [NE, D, FF], F32, kind="ExternalInput").ap()
    c_idbf = dt("c_idbf", [128, 128], BF16, kind="ExternalInput").ap()
    c_idf = dt("c_idf", [16, 16], F32, kind="ExternalInput").ap()
    c_onesrow = dt("c_onesrow", [1, 128], F32, kind="ExternalInput").ap()
    c_onescbf = dt("c_onescbf", [128, 1], BF16, kind="ExternalInput").ap()
    c_ones512 = dt("c_ones512", [128, 512], F32, kind="ExternalInput").ap()
    c_sel = dt("c_sel", [NH, 128], F32, kind="ExternalInput").ap()
    c_mask = dt("c_mask", [NH, NCB], F32, kind="ExternalInput").ap()

    out = dt("out", [TOKS, D], F32, kind="ExternalOutput").ap()

    xbf = dt("xbf", [TOKS, D], BF16, kind="Internal").ap()
    wqbf = dt("wqbf", [D, D], BF16, kind="Internal").ap()
    wvbf = dt("wvbf", [D, D], BF16, kind="Internal").ap()
    wgbf = dt("wgbf", [D, D], BF16, kind="Internal").ap()
    wobf = dt("wobf", [D, D], BF16, kind="Internal").ap()
    b1bf = dt("b1bf", [NE, FF], BF16, kind="Internal").ap()
    b2bf = dt("b2bf", [NE, D], BF16, kind="Internal").ap()
    w1bf = dt("w1bf", [NE, FF, D], BF16, kind="Internal").ap()
    w2bf = dt("w2bf", [NE, D, FF], BF16, kind="Internal").ap()
    w1t = dt("dbg_w1t" if debug else "w1t", [NE, D, FF], BF16,
             kind="ExternalOutput" if debug else "Internal").ap()
    w2t = dt("dbg_w2t" if debug else "w2t", [NE, FF, D], BF16,
             kind="ExternalOutput" if debug else "Internal").ap()
    hfp = dt("dbg_h" if debug else "hfp", [TOKS, D], F32,
             kind="ExternalOutput" if debug else "Internal").ap()
    hbf = dt("hbf", [TOKS, D], BF16, kind="Internal").ap()
    if debug:
        dbg_vt = dt("dbg_vt", [128, NCB * WAVE], F32, kind="ExternalOutput").ap()
        dbg_y = dt("dbg_y", [128, NCB * WAVE], F32, kind="ExternalOutput").ap()
        dbg_z = dt("dbg_z", [128, NCB * WAVE], BF16, kind="ExternalOutput").ap()
        dbg_lamb = dt("dbg_lamb", [128, NCB * WAVE], F32, kind="ExternalOutput").ap()
        dbg_xf = dt("dbg_xf", [128, TPW * D], F32, kind="ExternalOutput").ap()
        dbg_sm = dt("dbg_sm", [128, 16], F32, kind="ExternalOutput").ap()
        dbg_wn = dt("dbg_wn", [128, TPW * 16], F32, kind="ExternalOutput").ap()
        dbg_hid = dt("dbg_hid", [128, NFT * WAVE], BF16, kind="ExternalOutput").ap()
        dbg_acc = dt("dbg_acc", [128, TPW * D], F32, kind="ExternalOutput").ap()
        dbg_hfw = dt("dbg_hfw", [128, TPW * D], F32, kind="ExternalOutput").ap()
        dbg_ht = dt("dbg_ht", [128, 2 * NDC * WAVE], BF16, kind="ExternalOutput").ap()

    prog = Prog()
    P = prog

    with ExitStack() as outer:
        ent = outer.enter_context
        qPE = Ctr(ent(nc.semaphore("qPE")), 1)
        qDV = Ctr(ent(nc.semaphore("qDV")), 1)
        qSC = Ctr(ent(nc.semaphore("qSC")), 1)
        qSY = Ctr(ent(nc.semaphore("qSY")), 16)
        qAC = Ctr(ent(nc.semaphore("qAC")), 16)
        qGP = Ctr(ent(nc.semaphore("qGP")), 16)
        qSM = Ctr(ent(nc.semaphore("qSM")), 16)
        qCA = Ctr(ent(nc.semaphore("qCA")), 16)
        qCW1 = [Ctr(ent(nc.semaphore(f"qCWa{i}")), 16) for i in range(NE)]
        qCW2 = [Ctr(ent(nc.semaphore(f"qCWb{i}")), 16) for i in range(NE)]
        qB2 = Ctr(ent(nc.semaphore("qB2")), 16)
        qWT = Ctr(ent(nc.semaphore("qWT")), 16)
        qXT = [Ctr(ent(nc.semaphore("qXT0")), 16), Ctr(ent(nc.semaphore("qXT1")), 16)]
        qHT = [Ctr(ent(nc.semaphore("qHT0")), 16), Ctr(ent(nc.semaphore("qHT1")), 16)]
        qW1 = [Ctr(ent(nc.semaphore("qW1a")), 16), Ctr(ent(nc.semaphore("qW1b")), 16)]
        qXF = Ctr(ent(nc.semaphore("qXF")), 16)
        qHF = Ctr(ent(nc.semaphore("qHF")), 16)
        qHS = [Ctr(ent(nc.semaphore("qHS0")), 16), Ctr(ent(nc.semaphore("qHS1")), 16)]
        qBX = [Ctr(ent(nc.semaphore(f"qBX{i}")), 16) for i in range(4)]
        qBS = [Ctr(ent(nc.semaphore(f"qBS{i}")), 16) for i in range(4)]
        qw2g = [Ctr(ent(nc.semaphore(f"qw2g{i}")), 16) for i in range(4)]
        qOUT = [Ctr(ent(nc.semaphore("qOUT0")), 16), Ctr(ent(nc.semaphore("qOUT1")), 16)]

        ps = [ent(nc.psum_tensor(f"ps{i}", [128, 512], F32)) for i in range(8)]

        sb = nc.sbuf_tensor
        idbf = ent(sb("idbf", [128, 128], BF16))
        idf = ent(sb("idf", [16, 16], F32))
        onesrow = ent(sb("onesrow", [1, 128], F32))
        onescbf = ent(sb("onescbf", [128, 1], BF16))
        ones512 = ent(sb("ones512", [128, 512], F32))
        sel_sb = ent(sb("sel_sb", [NH, 128], F32))
        mask_sb = ent(sb("mask_sb", [NH, NCB], F32))
        lam_col = ent(sb("lam_col", [NH, 1], F32))
        lam_b = ent(sb("lam_b", [128, NCB], F32))
        gamc = ent(sb("gamc", [128, NCB], F32))
        betc = ent(sb("betc", [128, NCB], F32))
        rt_sb = ent(sb("rt_sb", [128, NDC * 16], BF16))
        rbq_bc = ent(sb("rbq_bc", [128, 16], F32))
        b2_sb = ent(sb("b2_sb", [NE, D], BF16))
        b1c = ent(sb("b1c", [128, NE * NFT], F32))
        smallA = ent(sb("smallA", [NH, D], F32))
        smallB = ent(sb("smallB", [NH, D], F32))
        smallC = ent(sb("smallC", [NH, D], F32))
        smallBf = ent(sb("smallBf", [NH, D], BF16))
        scol = ent(sb("scol", [NH, 2], F32))
        rrow = ent(sb("rrow", [1, 16], F32))
        b1cb = ent(sb("b1cb", [128, NE * NFT], BF16))
        bnc = ent(sb("bnc", [128, 4 * 1024], BF16))

        def vchain(build):
            v = P.op("vector", build, inc=qDV)
            P.wait("vector", qDV, v)
            return v

        for o, i in (
            (idbf[:, :], c_idbf[:, :]), (idf[:, :], c_idf[:, :]),
            (onesrow[:, :], c_onesrow[:, :]), (onescbf[:, :], c_onescbf[:, :]),
            (ones512[:, :], c_ones512[:, :]),
            (sel_sb[:, :], c_sel[:, :]), (mask_sb[:, :], c_mask[:, :]),
            (smallA[:, :], keys[:, :]), (smallB[:, :], rwt[:, :]),
            (scol[:, 0:1], rb[:, :]), (lam_col[:, :], beta[:, :]),
        ):
            P.op("gpsimd", lambda E, o=o, i=i: E.dma_start(out=o, in_=i), inc=qSM)
        for cb in range(NCB):
            P.op("gpsimd", lambda E, cb=cb: E.dma_start(
                out=gamc[:, cb:cb + 1], in_=gam[cb * 128:(cb + 1) * 128, :]), inc=qSM)
            P.op("gpsimd", lambda E, cb=cb: E.dma_start(
                out=betc[:, cb:cb + 1], in_=bet[cb * 128:(cb + 1) * 128, :]), inc=qSM)
        gp_small = qSM.n

        P.op("gpsimd", lambda E: E.dma_start(out=xbf[:, :], in_=x[:, :]), inc=qCA)
        for o, i in ((wqbf, wq), (wvbf, wv), (wgbf, wg), (wobf, wo)):
            P.op("gpsimd", lambda E, o=o, i=i: E.dma_start(out=o[:, :], in_=i[:, :]), inc=qCA)
        P.op("gpsimd", lambda E: E.dma_start(out=b1bf[:, :], in_=b1[:, :]), inc=qCA)
        P.op("gpsimd", lambda E: E.dma_start(out=b2bf[:, :], in_=b2[:, :]), inc=qCA)
        gp_cast_all = qCA.n
        P.wait("gpsimd", qCA, gp_cast_all)
        P.op("gpsimd", lambda E: E.dma_start(out=b2_sb[:, :], in_=b2bf[:, :]), inc=qB2)
        b2_done = qB2.n

        cast_emitted = set()

        def emit_cast(e_):
            if e_ in cast_emitted or e_ >= NE:
                return
            cast_emitted.add(e_)
            P.op("gpsimd", lambda E, e_=e_: E.dma_start(out=w1bf[e_, :, :], in_=w1[e_, :, :]), inc=qCW1[e_])
            P.op("gpsimd", lambda E, e_=e_: E.dma_start(out=w2bf[e_, :, :], in_=w2[e_, :, :]), inc=qCW2[e_])

        P.wait("scalar", qSM, gp_small)
        P.op("scalar", lambda E: E.activation(lam_col[:, :], lam_col[:, :], AF.Sigmoid), inc=qSC)
        sc_lam = qSC.n
        P.wait("vector", qSC, sc_lam)
        P.op("vector", lambda E: E.tensor_scalar_mul(mask_sb[:, :], mask_sb[:, :], lam_col[:, :]), inc=qDV)
        dv_x2 = qDV.n
        P.wait("tensor", qDV, dv_x2)
        P.op("tensor", lambda E: E.matmul(ps[6][0:128, 0:NCB], sel_sb[:, :], mask_sb[:, :],
                                          start=True, stop=True), inc=qPE)
        pe_lamb = qPE.n
        P.wait("vector", qPE, pe_lamb)
        P.op("vector", lambda E: E.tensor_copy(lam_b[:, :], ps[6][0:128, 0:NCB]), inc=qDV)
        dv_lamb = qDV.n

        P.wait("vector", qSM, gp_small)
        P.op("vector", lambda E: E.scalar_tensor_tensor(
            smallC[:, :], smallA[:, :], 2.0, smallB[:, :], ALU.mult, ALU.add))
        P.op("vector", lambda E: E.tensor_copy(smallBf[:, :], smallC[:, :]), inc=qDV)
        dv_rbf = qDV.n
        vchain(lambda E: E.tensor_mul(smallB[:, :], smallA[:, :], smallA[:, :]))
        vchain(lambda E: E.reduce_sum(scol[:, 1:2], smallB[:, :], axis=AX.X))
        dv_rbq = vchain(lambda E: E.tensor_sub(scol[:, 1:2], scol[:, 0:1], scol[:, 1:2]))
        P.wait("tensor", qDV, dv_rbf)
        prev_evac = 0
        for dc in range(NDC):
            P.wait("tensor", qDV, prev_evac)
            P.op("tensor", lambda E, dc=dc: E.matmul(
                ps[7][0:128, 0:16], smallBf[0:16, dc * 128:(dc + 1) * 128],
                idbf[0:16, 0:16], start=True, stop=True), inc=qPE)
            v = qPE.n
            P.wait("vector", qPE, v)
            prev_evac = P.op("vector", lambda E, dc=dc: E.tensor_copy(
                rt_sb[:, dc * 16:(dc + 1) * 16], ps[7][0:128, 0:16]), inc=qDV)
        dv_rt = prev_evac
        P.wait("tensor", qDV, dv_rbq)
        P.wait("tensor", qDV, dv_rt)
        P.op("tensor", lambda E: E.matmul(ps[7][0:1, 0:16], scol[0:16, 1:2], idf[0:16, 0:16],
                                          start=True, stop=True), inc=qPE)
        v = qPE.n
        P.wait("vector", qPE, v)
        P.op("vector", lambda E: E.tensor_copy(rrow[:, :], ps[7][0:1, 0:16]), inc=qDV)
        dv_rrow = qDV.n
        P.wait("tensor", qDV, dv_rrow)
        P.op("tensor", lambda E: E.matmul(ps[7][0:128, 0:16], onesrow[0:1, :], rrow[0:1, :],
                                          start=True, stop=True), inc=qPE)
        v = qPE.n
        P.wait("vector", qPE, v)
        P.op("vector", lambda E: E.tensor_copy(rbq_bc[:, :], ps[7][0:128, 0:16]), inc=qDV)
        dv_rbqbc = qDV.n

        P.wait("sync", qCA, gp_cast_all)
        b1v = b1bf.rearrange("e (r p) -> (e r) p", p=128)
        P.op("sync", lambda E: E.dma_start(out=b1cb[:, :], in_=b1v[:, :], transpose=True), inc=qWT)

        h_store_snap = [[0] * TPW for _ in range(NWAVES)]
        sc_h_snap = [[0] * TPW for _ in range(NWAVES)]

        spill_slot = [0]
        spill_started = set()
        spill_done = {}

        def spill_chunks_of(e_):
            ch = [("w1", e_, dc, fq) for dc in range(NDC) for fq in range(4)]
            ch += [("w2", e_, fc, 0) for fc in range(NFT)]
            return ch

        def emit_spill_chunk(chunk):
            kind, e_, i1, i2 = chunk
            slot = spill_slot[0] % 4
            spill_slot[0] += 1
            if (e_, kind) not in spill_started:
                spill_started.add((e_, kind))
                P.wait("sync", (qCW1 if kind == "w1" else qCW2)[e_], 16)
            P.wait("sync", qBS[slot], qBS[slot].n)
            if kind == "w1":
                dc, fq = i1, i2
                sap = w1bf[e_, fq * 1024:(fq + 1) * 1024, dc * 128:(dc + 1) * 128]
                dap = w1t[e_, dc * 128:(dc + 1) * 128, fq * 1024:(fq + 1) * 1024]
            else:
                fc = i1
                sap = w2bf[e_, :, fc * 128:(fc + 1) * 128]
                dap = w2t[e_, fc * 128:(fc + 1) * 128, :]
            P.op("sync", lambda E, sap=sap, slot=slot: E.dma_start(
                out=bnc[:, slot * 1024:(slot + 1) * 1024], in_=sap,
                transpose=True), inc=qBX[slot])
            P.wait("scalar", qBX[slot], qBX[slot].n)
            P.op("scalar", lambda E, dap=dap, slot=slot: E.dma_start(
                out=dap, in_=bnc[:, slot * 1024:(slot + 1) * 1024]), inc=qBS[slot])
            spill_done[e_] = [(qBS[s], qBS[s].n) for s in range(4)]

        with ExitStack() as attn_ctx:
            aent = attn_ctx.enter_context
            wqT = aent(sb("wqT", [128, NDC * D], BF16))
            wvT = aent(sb("wvT", [128, NDC * D], BF16))
            wgT = aent(sb("wgT", [128, NDC * D], BF16))
            woT = aent(sb("woT", [128, NDC * D], BF16))
            xt_b = aent(sb("xt_b", [128, 2 * NDC * WAVE], BF16))
            xf_b = aent(sb("xf_b", [128, TPW * D], F32))
            vT = aent(sb("vT", [128, NCB * WAVE], F32))
            Yb = aent(sb("Yb", [128, NCB * WAVE], F32))
            zb = aent(sb("zb", [128, NCB * WAVE], BF16))
            lamB = aent(sb("lamB", [128, NCB * WAVE], F32))
            ybfs = aent(sb("ybfs", [128, WAVE], BF16))
            y2fs = aent(sb("y2fs", [128, WAVE], BF16))
            gsil = aent(sb("gsil", [128, WAVE], F32))
            t1s = aent(sb("t1s", [128, WAVE], F32))
            hb_s = aent(sb("hb_s", [128, 2 * D], F32))
            carry = aent(sb("carry", [128, NCB], F32))
            murow = aent(sb("murow", [1, WAVE], F32))
            msrow = aent(sb("msrow", [1, WAVE], F32))
            varrow = aent(sb("varrow", [1, WAVE], F32))
            rstdrow = aent(sb("rstdrow", [1, WAVE], F32))
            s1c = aent(sb("s1c", [128, 8], F32))

            P.wait("sync", qCA, gp_cast_all)
            for wtn, srcw in ((wqT, wqbf), (wvT, wvbf), (wgT, wgbf), (woT, wobf)):
                for dc in range(NDC):
                    P.op("sync", lambda E, wtn=wtn, srcw=srcw, dc=dc: E.dma_start(
                        out=wtn[:, dc * D:(dc + 1) * D],
                        in_=srcw[:, dc * 128:(dc + 1) * 128], transpose=True), inc=qWT)
            sy_wt = qWT.n
            P.wait("vector", qWT, sy_wt)
            P.op("vector", lambda E: E.tensor_copy(b1c[:, :], b1cb[:, :]), inc=qDV)
            P.wait("scalar", qDV, dv_lamb)
            for cb in range(NCB):
                P.op("scalar", lambda E, cb=cb: E.mul(
                    lamB[:, cb * WAVE:(cb + 1) * WAVE], ones512[:, :],
                    lam_b[:, cb:cb + 1]), inc=qSC)
            sc_lamB = qSC.n

            n_attn_spill_exp = max(0, min(n_waves - 1, n_experts))

            projb_cons = [0, 0]
            projb_conseng = [None, None]
            aob_cons = [0, 0, 0, 0]
            st_cons = 0
            xt_done_pe = [0, 0]
            xf_done_dv = 0
            ybf_cons = [0, 0]
            gsil_cons = [0, 0]
            hb_cons = [0, 0]
            ao_end_pe = 0
            sc_h_prev_wave = 0
            carry_snap = [0] * NCB
            dXT = [0] * NWAVES
            dXF = [0] * NWAVES

            for w in range(n_waves):
                par = w % 2
                if w == 0:
                    P.wait("sync", qCA, gp_cast_all)
                P.wait("sync", qPE, xt_done_pe[par])
                for dc in range(NDC):
                    P.op("sync", lambda E, w=w, dc=dc, par=par: E.dma_start(
                        out=xt_b[:, (par * NDC + dc) * WAVE:(par * NDC + dc + 1) * WAVE],
                        in_=xbf[w * WAVE:(w + 1) * WAVE, dc * 128:(dc + 1) * 128],
                        transpose=True), inc=qXT[par])
                dXT[w] = qXT[par].n
                P.wait("gpsimd", qDV, xf_done_dv)
                xv = x[w * WAVE:(w + 1) * WAVE, :].rearrange("(t p) d -> p t d", p=128)
                P.op("gpsimd", lambda E, xv=xv, xf_b=xf_b: E.dma_start(
                    out=xf_b[:, :].rearrange("p (t d) -> p t d", t=TPW), in_=xv), inc=qXF)
                dXF[w] = qXF.n

                if w < n_experts:
                    emit_cast(w)
                if 0 < w <= n_attn_spill_exp:
                    for chunk in spill_chunks_of(w - 1):
                        emit_spill_chunk(chunk)

                P.wait("tensor", qXT[par], dXT[w])
                if w == 0:
                    P.wait("tensor", qWT, sy_wt)

                def proj_group(cb, wslab, bank, par=par):
                    if projb_conseng[bank] == "dv":
                        P.wait("tensor", qDV, projb_cons[bank])
                    elif projb_conseng[bank] == "sc":
                        P.wait("tensor", qSC, projb_cons[bank])
                    for dc in range(NDC):
                        P.op("tensor", lambda E, cb=cb, dc=dc, wslab=wslab, bank=bank, par=par:
                             E.matmul(ps[bank][:, 0:WAVE],
                                      wslab[:, dc * D + cb * 128: dc * D + (cb + 1) * 128],
                                      xt_b[:, (par * NDC + dc) * WAVE:(par * NDC + dc + 1) * WAVE],
                                      start=(dc == 0), stop=(dc == NDC - 1)),
                             inc=(qPE if dc == NDC - 1 else None))
                    return qPE.n

                for cb in range(NCB):
                    bank = cb % 2
                    g = proj_group(cb, wvT, bank)
                    P.wait("vector", qPE, g)
                    projb_cons[bank] = P.op("vector", lambda E, cb=cb, bank=bank: E.tensor_copy(
                        vT[:, cb * WAVE:(cb + 1) * WAVE], ps[bank][:, 0:WAVE]), inc=qDV)
                    projb_conseng[bank] = "dv"

                if w == 0:
                    P.wait("vector", qSC, sc_lamB)
                else:
                    P.wait("vector", qSC, sc_h_prev_wave)
                for cb in range(NCB):
                    init = carry[:, cb:cb + 1] if (w % 4) != 0 else 0.0
                    if (w % 4) != 0:
                        P.wait("vector", qDV, carry_snap[cb])
                    vchain(lambda E, cb=cb, init=init: E.tensor_tensor_scan(
                        Yb[:, cb * WAVE:(cb + 1) * WAVE],
                        lamB[:, cb * WAVE:(cb + 1) * WAVE],
                        vT[:, cb * WAVE:(cb + 1) * WAVE],
                        init, ALU.mult, ALU.add))
                    if (w % 4) != 3:
                        carry_snap[cb] = P.op("vector", lambda E, cb=cb: E.tensor_copy(
                            carry[:, cb:cb + 1],
                            Yb[:, (cb + 1) * WAVE - 1:(cb + 1) * WAVE]), inc=qDV)

                P.wait("tensor", qDV, dv_lamb)
                P.wait("tensor", qDV, dv_rbqbc)
                if w > 0:
                    P.wait("tensor", qDV, st_cons)
                for cb in range(NCB):
                    bank = cb % 2
                    sp = 0
                    g = proj_group(cb, wqT, bank)
                    P.wait("vector", qPE, g)
                    P.op("vector", lambda E, cb=cb, bank=bank: E.tensor_mul(
                        Yb[:, cb * WAVE:(cb + 1) * WAVE],
                        Yb[:, cb * WAVE:(cb + 1) * WAVE], ps[bank][:, 0:WAVE]), inc=qDV)
                    projb_cons[bank] = qDV.n
                    projb_conseng[bank] = "dv"
                    dv_y = qDV.n
                    P.wait("vector", qPE, ybf_cons[sp])
                    P.op("vector", lambda E, cb=cb, sp=sp: E.tensor_copy(
                        ybfs[:, sp * WAVE:(sp + 1) * WAVE],
                        Yb[:, cb * WAVE:(cb + 1) * WAVE]), inc=qDV)
                    dv_ybf = qDV.n
                    P.wait("scalar", qPE, ybf_cons[sp])
                    P.wait("scalar", qDV, dv_y)
                    P.op("scalar", lambda E, cb=cb, sp=sp: E.activation(
                        y2fs[:, sp * WAVE:(sp + 1) * WAVE],
                        Yb[:, cb * WAVE:(cb + 1) * WAVE], AF.Square), inc=qSC)
                    sc_y2 = qSC.n
                    P.wait("tensor", qDV, dv_ybf)
                    P.wait("tensor", qSC, sc_y2)
                    P.op("tensor", lambda E, cb=cb, sp=sp: E.matmul(
                        ps[6][0:1, 0:WAVE], onescbf[:, :],
                        ybfs[:, sp * WAVE:(sp + 1) * WAVE],
                        start=(cb == 0), stop=(cb == NCB - 1)))
                    ybf_cons[sp] = P.op("tensor", lambda E, cb=cb, sp=sp: E.matmul(
                        ps[7][0:1, 0:WAVE], onescbf[:, :],
                        y2fs[:, sp * WAVE:(sp + 1) * WAVE],
                        start=(cb == 0), stop=(cb == NCB - 1)), inc=qPE)
                pe_stats = qPE.n

                P.wait("vector", qPE, pe_stats)
                vchain(lambda E: E.tensor_scalar_mul(murow[:, :], ps[6][0:1, 0:WAVE], 1.0 / D))
                vchain(lambda E: E.tensor_scalar_mul(msrow[:, :], ps[7][0:1, 0:WAVE], 1.0 / D))
                vchain(lambda E: E.tensor_mul(varrow[:, :], murow[:, :], murow[:, :]))
                vchain(lambda E: E.tensor_sub(varrow[:, :], msrow[:, :], varrow[:, :]))
                dv_var = vchain(lambda E: E.tensor_scalar_add(varrow[:, :], varrow[:, :], EPS))
                P.wait("scalar", qDV, dv_var)
                P.op("scalar", lambda E: E.activation(rstdrow[:, :], varrow[:, :], AF.Ln))
                P.op("scalar", lambda E: E.activation(rstdrow[:, :], rstdrow[:, :], AF.Exp, scale=-0.5), inc=qSC)
                sc_rstd = qSC.n
                P.wait("tensor", qDV, dv_var)
                P.wait("tensor", qSC, sc_rstd)
                P.op("tensor", lambda E: E.matmul(ps[6][0:128, 0:WAVE], onesrow[0:1, :],
                                                  murow[0:1, :], start=True, stop=True))
                P.op("tensor", lambda E: E.matmul(ps[7][0:128, 0:WAVE], onesrow[0:1, :],
                                                  rstdrow[0:1, :], start=True, stop=True), inc=qPE)
                pe_bc = qPE.n

                if w > 0:
                    P.wait("vector", qPE, ao_end_pe)
                for cb in range(NCB):
                    bank = cb % 2
                    sp = 0
                    g = proj_group(cb, wgT, bank)
                    P.wait("scalar", qPE, g)
                    P.wait("scalar", qDV, gsil_cons[sp])
                    projb_cons[bank] = P.op("scalar", lambda E, sp=sp, bank=bank: E.activation(
                        gsil[:, sp * WAVE:(sp + 1) * WAVE], ps[bank][:, 0:WAVE], AF.Silu), inc=qSC)
                    projb_conseng[bank] = "sc"
                    sc_sil = qSC.n
                    P.wait("vector", qPE, pe_bc)
                    P.wait("vector", qSC, sc_sil)
                    P.op("vector", lambda E, cb=cb: E.tensor_sub(
                        t1s[:, :], Yb[:, cb * WAVE:(cb + 1) * WAVE], ps[6][0:128, 0:WAVE]))
                    P.op("vector", lambda E: E.tensor_mul(t1s[:, :], t1s[:, :], ps[7][0:128, 0:WAVE]))
                    P.op("vector", lambda E, cb=cb: E.tensor_scalar(
                        t1s[:, :], t1s[:, :], gamc[:, cb:cb + 1], betc[:, cb:cb + 1],
                        ALU.mult, ALU.add))
                    gsil_cons[sp] = P.op("vector", lambda E, cb=cb, sp=sp: E.tensor_mul(
                        zb[:, cb * WAVE:(cb + 1) * WAVE], t1s[:, :],
                        gsil[:, sp * WAVE:(sp + 1) * WAVE]), inc=qDV)
                z_done_dv = qDV.n
                st_cons = z_done_dv

                P.wait("tensor", qDV, z_done_dv)
                for tt in range(TPW):
                    bpair = (2 + 2 * (tt % 2), 3 + 2 * (tt % 2))
                    P.wait("tensor", qDV, aob_cons[bpair[0] - 2])
                    P.wait("tensor", qDV, aob_cons[bpair[1] - 2])
                    for cb in range(NCB):
                        for h in range(2):
                            P.op("tensor", lambda E, cb=cb, tt=tt, h=h, b=bpair[h]:
                                 E.matmul(ps[b][:, 0:WAVE],
                                          zb[:, cb * WAVE + tt * 128: cb * WAVE + (tt + 1) * 128],
                                          woT[:, cb * D + h * WAVE: cb * D + (h + 1) * WAVE],
                                          start=(cb == 0), stop=(cb == NCB - 1)),
                                 inc=(qPE if (cb == NCB - 1 and h == 1) else None))
                    pe_ao = qPE.n
                    P.wait("vector", qPE, pe_ao)
                    P.wait("vector", qXF, dXF[w])
                    if tt > 0:
                        P.wait("vector", qSC, sc_h_snap[w][tt - 1])
                    P.op("vector", lambda E, tt=tt, b=bpair[0]: E.tensor_add(
                        Yb[:, 0:WAVE], xf_b[:, tt * D: tt * D + WAVE], ps[b][:, 0:WAVE]))
                    P.op("vector", lambda E, tt=tt, b=bpair[1]: E.tensor_add(
                        Yb[:, WAVE:D], xf_b[:, tt * D + WAVE: (tt + 1) * D], ps[b][:, 0:WAVE]), inc=qDV)
                    aob_cons[bpair[0] - 2] = qDV.n
                    aob_cons[bpair[1] - 2] = qDV.n
                    vchain(lambda E: E.reduce_sum(s1c[:, 0:1], Yb[:, 0:D], axis=AX.X))
                    vchain(lambda E: E.tensor_mul(Yb[:, D:2 * D], Yb[:, 0:D], Yb[:, 0:D]))
                    vchain(lambda E: E.reduce_sum(s1c[:, 1:2], Yb[:, D:2 * D], axis=AX.X))
                    vchain(lambda E: E.tensor_scalar_mul(s1c[:, 2:3], s1c[:, 0:1], 1.0 / D))
                    vchain(lambda E: E.tensor_scalar_mul(s1c[:, 3:4], s1c[:, 1:2], 1.0 / D))
                    vchain(lambda E: E.tensor_mul(s1c[:, 4:5], s1c[:, 2:3], s1c[:, 2:3]))
                    vchain(lambda E: E.tensor_sub(s1c[:, 4:5], s1c[:, 3:4], s1c[:, 4:5]))
                    dv_ln1 = vchain(lambda E: E.tensor_scalar_add(s1c[:, 4:5], s1c[:, 4:5], EPS))
                    P.wait("scalar", qDV, dv_ln1)
                    P.op("scalar", lambda E: E.activation(s1c[:, 5:6], s1c[:, 4:5], AF.Ln))
                    P.op("scalar", lambda E: E.activation(s1c[:, 5:6], s1c[:, 5:6], AF.Exp, scale=-0.5), inc=qSC)
                    sc_r = qSC.n
                    P.wait("vector", qSC, sc_r)
                    vchain(lambda E: E.tensor_mul(s1c[:, 6:7], s1c[:, 2:3], s1c[:, 5:6]))
                    dv_nb = vchain(lambda E: E.tensor_scalar_mul(s1c[:, 6:7], s1c[:, 6:7], -1.0))
                    hpar = tt % 2
                    P.wait("scalar", qDV, dv_nb)
                    P.wait("scalar", qHS[hpar], hb_cons[hpar])
                    sc_h_snap[w][tt] = P.op("scalar", lambda E, hpar=hpar: E.activation(
                        hb_s[:, hpar * D:(hpar + 1) * D], Yb[:, 0:D], AF.Identity,
                        bias=s1c[:, 6:7], scale=s1c[:, 5:6]), inc=qSC)
                    P.wait("gpsimd", qSC, sc_h_snap[w][tt])
                    r0 = w * WAVE + tt * 128
                    P.op("gpsimd", lambda E, r0=r0, hpar=hpar: E.dma_start(
                        out=hfp[r0:r0 + 128, :], in_=hb_s[:, hpar * D:(hpar + 1) * D]), inc=qHS[hpar])
                    P.op("gpsimd", lambda E, r0=r0, hpar=hpar: E.dma_start(
                        out=hbf[r0:r0 + 128, :], in_=hb_s[:, hpar * D:(hpar + 1) * D]), inc=qHS[hpar])
                    hb_cons[hpar] = qHS[hpar].n
                    h_store_snap[w][tt] = (qHS[0].n, qHS[1].n)
                ao_end_pe = qPE.n
                sc_h_prev_wave = sc_h_snap[w][TPW - 1]
                xt_done_pe[par] = qPE.n
                xf_done_dv = qDV.n

            if debug:
                P.wait("gpsimd", qDV, qDV.n)
                P.wait("gpsimd", qSC, qSC.n)
                P.wait("gpsimd", qPE, qPE.n)
                for o, i in ((dbg_vt, vT), (dbg_y, Yb), (dbg_z, zb),
                             (dbg_lamb, lamB), (dbg_xf, xf_b)):
                    P.op("gpsimd", lambda E, o=o, i=i: E.dma_start(out=o[:, :], in_=i[:, :]), inc=qGP)
                P.op("gpsimd", lambda E: E.dma_start(out=dbg_sm[:, 0:NCB], in_=lam_b[:, :]), inc=qGP)
                P.op("gpsimd", lambda E: E.dma_start(out=dbg_sm[:, NCB:2 * NCB], in_=gamc[:, :]), inc=qGP)

            attn_end = [(qPE, qPE.n), (qDV, qDV.n), (qSC, qSC.n),
                        (qSY, qSY.n), (qGP, qGP.n), (qSM, qSM.n), (qCA, qCA.n),
                        (qB2, qB2.n), (qWT, qWT.n), (qXT[0], qXT[0].n),
                        (qXT[1], qXT[1].n), (qXF, qXF.n),
                        (qHS[0], qHS[0].n), (qHS[1], qHS[1].n)]

        with ExitStack() as ffn_ctx:
            fent = ffn_ctx.enter_context
            ht_b = fent(sb("ht_b", [128, 2 * NDC * WAVE], BF16))
            w1g = fent(sb("w1g", [128, 2 * NDC * 1024], BF16))
            w2r = fent(sb("w2r", [128, 32 * WAVE], BF16))
            hid = fent(sb("hid", [128, NFT * WAVE], BF16))
            accb = fent(sb("accb", [128, TPW * D], F32))
            hfw = fent(sb("hfw", [128, TPW * D], F32))
            outs = fent(sb("outs", [128, 2 * D], F32))
            wnw = fent(sb("wnw", [128, TPW * 16], F32))
            wnbf = fent(sb("wnbf", [128, 16], BF16))
            wnT = fent(sb("wnT", [16, TPW * 128], BF16))
            lgs = fent(sb("lgs", [128, 16], F32))
            l2s = fent(sb("l2s", [128, 16], F32))
            iss = fent(sb("iss", [128, 16], F32))
            exs = fent(sb("exs", [128, 16], F32))
            mcol = fent(sb("mcol", [128, 8], F32))
            e1c = fent(sb("e1c", [128, 8], F32))

            for eng in Prog.ENGS:
                for ctr_, val_ in attn_end:
                    P.wait(eng, ctr_, val_)

            ht_done_pe = [0, 0]
            hfw_done_dv = 0
            outs_done = [0, 0]
            w1g_done_pe = [0, 0]
            w2grp_done_pe = [0] * 4
            f1_done_sc = [0, 0]
            f2_done_dv = [0, 0, 0, 0]
            lg_done_dv = 0
            wnT_done_dv = 0
            wnbf_guard_pe = 0
            sc_ex_prev = 0
            ep_act_sc = [0] * TPW

            for w in range(n_waves):
                par = w % 2
                P.wait("sync", qPE, ht_done_pe[par])
                for dc in range(NDC):
                    P.op("sync", lambda E, w=w, dc=dc, par=par: E.dma_start(
                        out=ht_b[:, (par * NDC + dc) * WAVE:(par * NDC + dc + 1) * WAVE],
                        in_=hbf[w * WAVE:(w + 1) * WAVE, dc * 128:(dc + 1) * 128],
                        transpose=True), inc=qHT[par])
                dHT = qHT[par].n
                P.wait("gpsimd", qDV, hfw_done_dv)
                hv = hfp[w * WAVE:(w + 1) * WAVE, :].rearrange("(t p) d -> p t d", p=128)
                P.op("gpsimd", lambda E, hv=hv, hfw=hfw: E.dma_start(
                    out=hfw[:, :].rearrange("p (t d) -> p t d", t=TPW), in_=hv), inc=qHF)
                dHF = qHF.n

                P.wait("tensor", qHT[par], dHT)
                for tt in range(TPW):
                    P.wait("tensor", qDV, lg_done_dv)
                    for dc in range(NDC):
                        P.op("tensor", lambda E, dc=dc, tt=tt, par=par: E.matmul(
                            ps[6][:, 0:16],
                            ht_b[:, (par * NDC + dc) * WAVE + tt * 128:
                                 (par * NDC + dc) * WAVE + (tt + 1) * 128],
                            rt_sb[:, dc * 16:(dc + 1) * 16],
                            start=(dc == 0), stop=(dc == NDC - 1)),
                            inc=(qPE if dc == NDC - 1 else None))
                    pe_lg = qPE.n
                    P.wait("vector", qPE, pe_lg)
                    P.wait("vector", qSC, sc_ex_prev)
                    lg_done_dv = vchain(lambda E: E.tensor_add(lgs[:, :], ps[6][:, 0:16], rbq_bc[:, :]))
                    vchain(lambda E: E.reduce_max(mcol[:, 0:1], lgs[:, :], axis=AX.X))
                    vchain(lambda E: E.tensor_single_scalar(iss[:, :], lgs[:, :], mcol[:, 0:1], ALU.is_ge))
                    vchain(lambda E: E.scalar_tensor_tensor(l2s[:, :], iss[:, :], -BIG, lgs[:, :], ALU.mult, ALU.add))
                    for k in (1, 2):
                        vchain(lambda E, k=k: E.reduce_max(mcol[:, k:k + 1], l2s[:, :], axis=AX.X))
                        vchain(lambda E, k=k: E.tensor_single_scalar(iss[:, :], l2s[:, :], mcol[:, k:k + 1], ALU.is_ge))
                        vchain(lambda E: E.scalar_tensor_tensor(l2s[:, :], iss[:, :], -BIG, l2s[:, :], ALU.mult, ALU.add))
                    vchain(lambda E: E.reduce_max(mcol[:, 3:4], l2s[:, :], axis=AX.X))
                    vchain(lambda E: E.tensor_single_scalar(iss[:, :], lgs[:, :], mcol[:, 3:4], ALU.is_ge))
                    dv_top = vchain(lambda E: E.tensor_scalar_mul(mcol[:, 4:5], mcol[:, 0:1], -1.0))
                    P.wait("scalar", qDV, dv_top)
                    P.op("scalar", lambda E: E.activation(exs[:, :], lgs[:, :], AF.Exp, bias=mcol[:, 4:5]), inc=qSC)
                    sc_ex = qSC.n
                    sc_ex_prev = sc_ex
                    P.wait("vector", qSC, sc_ex)
                    vchain(lambda E: E.tensor_mul(exs[:, :], exs[:, :], iss[:, :]))
                    vchain(lambda E: E.reduce_sum(mcol[:, 5:6], exs[:, :], axis=AX.X))
                    vchain(lambda E: E.reciprocal(mcol[:, 6:7], mcol[:, 5:6]))
                    vchain(lambda E, tt=tt: E.tensor_scalar_mul(
                        wnw[:, tt * 16:(tt + 1) * 16], exs[:, :], mcol[:, 6:7]))
                    P.wait("vector", qPE, wnbf_guard_pe)
                    P.op("vector", lambda E, tt=tt: E.tensor_copy(
                        wnbf[:, :], wnw[:, tt * 16:(tt + 1) * 16]), inc=qDV)
                    dv_wnbf = qDV.n
                    P.wait("tensor", qDV, dv_wnbf)
                    if wnT_done_dv:
                        P.wait("tensor", qDV, wnT_done_dv)
                    P.op("tensor", lambda E: E.matmul(ps[7][0:16, 0:128], wnbf[:, :], idbf[:, :],
                                                      start=True, stop=True), inc=qPE)
                    pe_wnt = qPE.n
                    wnbf_guard_pe = pe_wnt
                    P.wait("vector", qPE, pe_wnt)
                    P.op("vector", lambda E, tt=tt: E.tensor_copy(
                        wnT[0:16, tt * 128:(tt + 1) * 128], ps[7][0:16, 0:128]), inc=qDV)
                    wnT_done_dv = qDV.n

                P.wait("tensor", qDV, wnT_done_dv)
                if w == 0:
                    P.wait("tensor", qB2, b2_done)
                for tt in range(TPW):
                    bk = (2 + 2 * (tt % 2), 3 + 2 * (tt % 2))
                    P.wait("tensor", qDV, f2_done_dv[bk[0] - 2])
                    P.wait("tensor", qDV, f2_done_dv[bk[1] - 2])
                    for h in range(2):
                        P.op("tensor", lambda E, tt=tt, h=h, b=bk[h]: E.matmul(
                            ps[b][:, 0:WAVE], wnT[0:16, tt * 128:(tt + 1) * 128],
                            b2_sb[0:16, h * WAVE:(h + 1) * WAVE],
                            start=True, stop=True), inc=(qPE if h == 1 else None))
                    pe_bias = qPE.n
                    P.wait("vector", qPE, pe_bias)
                    P.wait("vector", qHF, dHF)
                    if ep_act_sc[tt]:
                        P.wait("vector", qSC, ep_act_sc[tt])
                    for h in range(2):
                        P.op("vector", lambda E, tt=tt, h=h, b=bk[h]: E.scalar_tensor_tensor(
                            accb[:, tt * D + h * WAVE: tt * D + (h + 1) * WAVE],
                            ps[b][:, 0:WAVE], 1.0,
                            hfw[:, tt * D + h * WAVE: tt * D + (h + 1) * WAVE],
                            ALU.mult, ALU.add), inc=qDV)
                    f2_done_dv[bk[0] - 2] = qDV.n
                    f2_done_dv[bk[1] - 2] = qDV.n
                hfw_done_dv = qDV.n

                groups_seq = [(e_, fg_) for e_ in range(n_experts) for fg_ in range(4)]
                dW1_snap = {}

                def emit_w1_load(gidx, w=w):
                    e_, fg_ = groups_seq[gidx]
                    gpar = gidx % 2
                    P.wait("sync", qPE, w1g_done_pe[gpar])
                    if w == 0:
                        if e_ not in spill_done:
                            for chunk in spill_chunks_of(e_):
                                emit_spill_chunk(chunk)
                        for ctr_, val_ in spill_done[e_]:
                            P.wait("sync", ctr_, val_)
                    P.op("sync", lambda E, e_=e_, fg_=fg_, gpar=gpar: E.dma_start(
                        out=w1g[:, gpar * NDC * 1024:(gpar + 1) * NDC * 1024]
                            .rearrange("p (j f) -> p j f", j=NDC),
                        in_=w1t[e_].rearrange("(j p) f -> p j f", p=128)
                            [:, :, fg_ * 1024:(fg_ + 1) * 1024]), inc=qW1[gpar])
                    dW1_snap[gidx] = (gpar, qW1[gpar].n)

                emit_w1_load(0)
                if len(groups_seq) > 1:
                    emit_w1_load(1)
                for e in range(n_experts):
                    dW2g = [[0] * 4, [0] * 4]
                    if w == 0:
                        if e not in spill_done:
                            for chunk in spill_chunks_of(e):
                                emit_spill_chunk(chunk)
                        for ctr_, val_ in spill_done[e]:
                            P.wait("scalar", ctr_, val_)
                    for g8 in range(4):
                        P.wait("scalar", qPE, w2grp_done_pe[g8])
                        dW2g[0][g8] = P.op("scalar", lambda E, e=e, g8=g8: E.dma_start(
                            out=w2r[:, g8 * 8 * WAVE:(g8 + 1) * 8 * WAVE]
                                .rearrange("p (j n) -> p j n", j=8),
                            in_=w2t[e].rearrange("(j p) d -> p j d", p=128)
                                [:, g8 * 8:(g8 + 1) * 8, 0:WAVE]), inc=qw2g[g8])

                    sc_gelu = [0] * NFT
                    for ft in range(NFT):
                        fg, ftl = divmod(ft, 8)
                        bank = ft % 2
                        gidx = e * 4 + fg
                        gpar = gidx % 2
                        if ftl == 0 and gidx in dW1_snap:
                            P.wait("tensor", qW1[dW1_snap[gidx][0]], dW1_snap[gidx][1])
                        if w == 0 and e == 0 and ft == 0:
                            P.wait("tensor", qHT[par], dHT)
                        P.wait("tensor", qSC, f1_done_sc[bank])
                        for dc in range(NDC):
                            P.op("tensor", lambda E, ftl=ftl, dc=dc, bank=bank, gpar=gpar, par=par:
                                 E.matmul(ps[bank][:, 0:WAVE],
                                          w1g[:, (gpar * NDC + dc) * 1024 + ftl * 128:
                                              (gpar * NDC + dc) * 1024 + (ftl + 1) * 128],
                                          ht_b[:, (par * NDC + dc) * WAVE:(par * NDC + dc + 1) * WAVE],
                                          start=(dc == 0), stop=(dc == NDC - 1)),
                                 inc=(qPE if dc == NDC - 1 else None))
                        pe_f1 = qPE.n
                        if w == 0:
                            se = n_attn_spill_exp + e
                            if se < n_experts:
                                ch = spill_chunks_of(se)
                                for chunk in ch[ft * 2:(ft + 1) * 2]:
                                    emit_spill_chunk(chunk)
                        if ftl == 7:
                            w1g_done_pe[gpar] = pe_f1
                            if gidx + 2 < len(groups_seq):
                                emit_w1_load(gidx + 2)
                            if w == 0 and fg == 0:
                                emit_cast(n_waves + e)
                        P.wait("scalar", qPE, pe_f1)
                        bcol = e * NFT + ft
                        f1_done_sc[bank] = P.op("scalar", lambda E, ft=ft, bank=bank, bcol=bcol:
                                                E.activation(hid[:, ft * WAVE:(ft + 1) * WAVE],
                                                             ps[bank][:, 0:WAVE], AF.Gelu,
                                                             bias=b1c[:, bcol:bcol + 1]), inc=qSC)
                        sc_gelu[ft] = f1_done_sc[bank]

                    for dh in range(2):
                        for fc in range(32):
                            if fc % 8 == 0:
                                P.wait("tensor", qw2g[fc // 8], dW2g[dh][fc // 8])
                            P.wait("tensor", qSC, sc_gelu[fc])
                            for tt in range(TPW):
                                b = 2 + tt
                                if fc == 0:
                                    P.wait("tensor", qDV, f2_done_dv[tt])
                                P.op("tensor", lambda E, fc=fc, tt=tt, b=b:
                                     E.matmul(ps[b][:, 0:WAVE],
                                              hid[:, fc * WAVE + tt * 128: fc * WAVE + (tt + 1) * 128],
                                              w2r[:, fc * WAVE:(fc + 1) * WAVE],
                                              start=(fc == 0), stop=(fc == 31)),
                                     inc=(qPE if tt == TPW - 1 else None))
                            if fc % 8 == 7:
                                g8 = fc // 8
                                w2grp_done_pe[g8] = qPE.n
                                if dh == 0:
                                    P.wait("scalar", qPE, w2grp_done_pe[g8])
                                    dW2g[1][g8] = P.op("scalar", lambda E, e=e, g8=g8: E.dma_start(
                                        out=w2r[:, g8 * 8 * WAVE:(g8 + 1) * 8 * WAVE]
                                            .rearrange("p (j n) -> p j n", j=8),
                                        in_=w2t[e].rearrange("(j p) d -> p j d", p=128)
                                            [:, g8 * 8:(g8 + 1) * 8, WAVE:2 * WAVE]), inc=qw2g[g8])
                        pe_f2 = qPE.n
                        for tt in range(TPW):
                            b = 2 + tt
                            P.wait("vector", qPE, pe_f2)
                            P.op("vector", lambda E, e=e, dh=dh, tt=tt, b=b: E.scalar_tensor_tensor(
                                accb[:, tt * D + dh * WAVE: tt * D + (dh + 1) * WAVE],
                                ps[b][:, 0:WAVE], wnw[:, tt * 16 + e: tt * 16 + e + 1],
                                accb[:, tt * D + dh * WAVE: tt * D + (dh + 1) * WAVE],
                                ALU.mult, ALU.add), inc=qDV)
                            f2_done_dv[tt] = qDV.n

                for tt in range(TPW):
                    opar = tt % 2
                    P.wait("vector", qOUT[opar], outs_done[opar])
                    vchain(lambda E, tt=tt: E.reduce_sum(
                        e1c[:, 0:1], accb[:, tt * D:(tt + 1) * D], axis=AX.X))
                    vchain(lambda E, tt=tt, opar=opar: E.tensor_mul(
                        outs[:, opar * D:(opar + 1) * D],
                        accb[:, tt * D:(tt + 1) * D], accb[:, tt * D:(tt + 1) * D]))
                    vchain(lambda E, opar=opar: E.reduce_sum(
                        e1c[:, 1:2], outs[:, opar * D:(opar + 1) * D], axis=AX.X))
                    vchain(lambda E: E.tensor_scalar_mul(e1c[:, 2:3], e1c[:, 0:1], 1.0 / D))
                    vchain(lambda E: E.tensor_scalar_mul(e1c[:, 3:4], e1c[:, 1:2], 1.0 / D))
                    vchain(lambda E: E.tensor_mul(e1c[:, 4:5], e1c[:, 2:3], e1c[:, 2:3]))
                    vchain(lambda E: E.tensor_sub(e1c[:, 4:5], e1c[:, 3:4], e1c[:, 4:5]))
                    dv_ln2 = vchain(lambda E: E.tensor_scalar_add(e1c[:, 4:5], e1c[:, 4:5], EPS))
                    P.wait("scalar", qDV, dv_ln2)
                    P.op("scalar", lambda E: E.activation(e1c[:, 5:6], e1c[:, 4:5], AF.Ln))
                    P.op("scalar", lambda E: E.activation(e1c[:, 5:6], e1c[:, 5:6], AF.Exp, scale=-0.5), inc=qSC)
                    sc_r2 = qSC.n
                    P.wait("vector", qSC, sc_r2)
                    vchain(lambda E: E.tensor_mul(e1c[:, 6:7], e1c[:, 2:3], e1c[:, 5:6]))
                    dv_nb2 = vchain(lambda E: E.tensor_scalar_mul(e1c[:, 6:7], e1c[:, 6:7], -1.0))
                    P.wait("scalar", qDV, dv_nb2)
                    ep_act_sc[tt] = P.op("scalar", lambda E, tt=tt, opar=opar: E.activation(
                        outs[:, opar * D:(opar + 1) * D], accb[:, tt * D:(tt + 1) * D],
                        AF.Identity, bias=e1c[:, 6:7], scale=e1c[:, 5:6]), inc=qSC)
                    P.wait("gpsimd", qSC, ep_act_sc[tt])
                    r0 = w * WAVE + tt * 128
                    P.op("gpsimd", lambda E, r0=r0, opar=opar: E.dma_start(
                        out=out[r0:r0 + 128, :], in_=outs[:, opar * D:(opar + 1) * D]), inc=qOUT[opar])
                    outs_done[opar] = qOUT[opar].n
                ht_done_pe[par] = qPE.n

            if debug:
                P.wait("gpsimd", qDV, qDV.n)
                P.wait("gpsimd", qSC, qSC.n)
                P.wait("gpsimd", qPE, qPE.n)
                for o, i in ((dbg_wn, wnw), (dbg_hid, hid), (dbg_acc, accb),
                             (dbg_hfw, hfw), (dbg_ht, ht_b)):
                    P.op("gpsimd", lambda E, o=o, i=i: E.dma_start(out=o[:, :], in_=i[:, :]), inc=qGP)

            P.wait("gpsimd", qOUT[0], qOUT[0].n)
            P.wait("gpsimd", qOUT[1], qOUT[1].n)

            block = outer.enter_context(nc.Block())
            streams = prog.streams

            @block.tensor
            def _(eng):
                for fn in streams["tensor"]:
                    fn(eng)

            @block.vector
            def _(eng):
                for fn in streams["vector"]:
                    fn(eng)

            @block.scalar
            def _(eng):
                for fn in streams["scalar"]:
                    fn(eng)

            @block.sync
            def _(eng):
                for fn in streams["sync"]:
                    fn(eng)

            @block.gpsimd
            def _(eng):
                for fn in streams["gpsimd"]:
                    fn(eng)

    return nc


def _consts():
    sel = np.zeros((NH, 128), np.float32)
    for h in range(NH):
        sel[h, (h % 2) * 64:(h % 2) * 64 + 64] = 1.0
    mask = np.zeros((NH, NCB), np.float32)
    for h in range(NH):
        mask[h, h // 2] = 1.0
    return {
        "c_idbf": np.eye(128, dtype=BFNP),
        "c_idf": np.eye(16, dtype=np.float32),
        "c_onesrow": np.ones((1, 128), np.float32),
        "c_onescbf": np.ones((128, 1), BFNP),
        "c_ones512": np.ones((128, 512), np.float32),
        "c_sel": sel,
        "c_mask": mask,
    }


def make_in_maps(inputs):
    xs = np.ascontiguousarray(np.asarray(inputs["L0_attn_x"], dtype=np.float32)).reshape(8, TOKS, D)
    base = {
        "wq": np.asarray(inputs["L0_attn_Wq"], np.float32),
        "wv": np.asarray(inputs["L0_attn_Wv"], np.float32),
        "wg": np.asarray(inputs["L0_attn_Wg"], np.float32),
        "wo": np.asarray(inputs["L0_attn_Wo"], np.float32),
        "beta": np.asarray(inputs["L0_attn_beta"], np.float32).reshape(NH, 1),
        "gam": np.asarray(inputs["L0_attn_ln_gamma"], np.float32).reshape(D, 1),
        "bet": np.asarray(inputs["L0_attn_ln_beta"], np.float32).reshape(D, 1),
        "keys": np.asarray(inputs["L0_ffn_keys"], np.float32),
        "rwt": np.asarray(inputs["L0_ffn_rw"], np.float32),
        "rb": np.asarray(inputs["L0_ffn_rb"], np.float32).reshape(NE, 1),
        "b1": np.asarray(inputs["L0_ffn_b1"], np.float32),
        "b2": np.asarray(inputs["L0_ffn_b2"], np.float32),
        "w1": np.asarray(inputs["L0_ffn_w1"], np.float32),
        "w2": np.asarray(inputs["L0_ffn_w2"], np.float32),
    }
    base.update(_consts())
    return [dict(base, x=xs[c]) for c in range(8)]


def kernel(L0_attn_x, L0_attn_Wq, L0_attn_Wv, L0_attn_Wo, L0_attn_Wg,
           L0_attn_beta, L0_attn_ln_gamma, L0_attn_ln_beta,
           L0_ffn_keys, L0_ffn_w1, L0_ffn_b1, L0_ffn_w2, L0_ffn_b2,
           L0_ffn_rw, L0_ffn_rb):
    nc = build_program()
    in_maps = make_in_maps(dict(
        L0_attn_x=L0_attn_x, L0_attn_Wq=L0_attn_Wq, L0_attn_Wv=L0_attn_Wv,
        L0_attn_Wo=L0_attn_Wo, L0_attn_Wg=L0_attn_Wg, L0_attn_beta=L0_attn_beta,
        L0_attn_ln_gamma=L0_attn_ln_gamma, L0_attn_ln_beta=L0_attn_ln_beta,
        L0_ffn_keys=L0_ffn_keys, L0_ffn_w1=L0_ffn_w1, L0_ffn_b1=L0_ffn_b1,
        L0_ffn_w2=L0_ffn_w2, L0_ffn_b2=L0_ffn_b2, L0_ffn_rw=L0_ffn_rw,
        L0_ffn_rb=L0_ffn_rb))
    res = run_bass_kernel_spmd(nc, in_maps, core_ids=list(range(8)))
    o = np.concatenate([np.asarray(r["out"], np.float32) for r in res.results], axis=0)
    return o.reshape(16, 2048, D)


# version 07
# speedup vs baseline: 1.1110x; 1.1110x over previous
import numpy as np
import ml_dtypes
from contextlib import ExitStack

import concourse.bass as bass
import concourse.mybir as mybir
from concourse.bass_utils import run_bass_kernel_spmd

F32 = mybir.dt.float32
BF16 = mybir.dt.bfloat16
AF = mybir.ActivationFunctionType
ALU = mybir.AluOpType
AX = mybir.AxisListType
BFNP = ml_dtypes.bfloat16

D = 1024
NH = 16
NE = 16
FF = 4096
TOKS = 4096
WAVE = 512
NWAVES = TOKS // WAVE
TPW = WAVE // 128
NCB = D // 128
NDC = D // 128
NFT = FF // 128
EPS = 1e-5
BIG = 1.0e38


class Ctr:
    __slots__ = ("h", "n", "step")

    def __init__(self, h, step=1):
        self.h = h
        self.n = 0
        self.step = step

    def bump(self):
        self.n += self.step
        return self.n


class Prog:
    ENGS = ("tensor", "vector", "scalar", "sync", "gpsimd")

    def __init__(self):
        self.streams = {k: [] for k in self.ENGS}

    def wait(self, eng, ctr, val):
        if val is None or val <= 0:
            return
        h = ctr.h
        self.streams[eng].append(lambda E, h=h, v=val: E.wait_ge(h, v))

    def op(self, eng, build, inc=None):
        if inc is not None:
            tgt = inc.bump()
            h, s = inc.h, inc.step

            def fn(E, build=build, h=h, s=s):
                build(E).then_inc(h, s)
        else:
            tgt = None

            def fn(E, build=build):
                build(E)
        self.streams[eng].append(fn)
        return tgt


def build_program(n_waves=NWAVES, n_experts=NE, debug=False, diag_serial_w1=False):
    nc = bass.Bass("TRN2", target_bir_lowering=False)
    dt = nc.dram_tensor

    x = dt("x", [TOKS, D], F32, kind="ExternalInput").ap()
    wq = dt("wq", [D, D], F32, kind="ExternalInput").ap()
    wv = dt("wv", [D, D], F32, kind="ExternalInput").ap()
    wg = dt("wg", [D, D], F32, kind="ExternalInput").ap()
    wo = dt("wo", [D, D], F32, kind="ExternalInput").ap()
    beta = dt("beta", [NH, 1], F32, kind="ExternalInput").ap()
    gam = dt("gam", [D, 1], F32, kind="ExternalInput").ap()
    bet = dt("bet", [D, 1], F32, kind="ExternalInput").ap()
    keys = dt("keys", [NE, D], F32, kind="ExternalInput").ap()
    rwt = dt("rwt", [NE, D], F32, kind="ExternalInput").ap()
    rb = dt("rb", [NE, 1], F32, kind="ExternalInput").ap()
    b1 = dt("b1", [NE, FF], F32, kind="ExternalInput").ap()
    b2 = dt("b2", [NE, D], F32, kind="ExternalInput").ap()
    w1 = dt("w1", [NE, FF, D], F32, kind="ExternalInput").ap()
    w2 = dt("w2", [NE, D, FF], F32, kind="ExternalInput").ap()
    c_idbf = dt("c_idbf", [128, 128], BF16, kind="ExternalInput").ap()
    c_idf = dt("c_idf", [16, 16], F32, kind="ExternalInput").ap()
    c_onesrow = dt("c_onesrow", [1, 128], F32, kind="ExternalInput").ap()
    c_onescbf = dt("c_onescbf", [128, 1], BF16, kind="ExternalInput").ap()
    c_ones512 = dt("c_ones512", [128, 512], F32, kind="ExternalInput").ap()
    c_sel = dt("c_sel", [NH, 128], F32, kind="ExternalInput").ap()
    c_mask = dt("c_mask", [NH, NCB], F32, kind="ExternalInput").ap()

    out = dt("out", [TOKS, D], F32, kind="ExternalOutput").ap()

    xbf = dt("xbf", [TOKS, D], BF16, kind="Internal").ap()
    wqbf = dt("wqbf", [D, D], BF16, kind="Internal").ap()
    wvbf = dt("wvbf", [D, D], BF16, kind="Internal").ap()
    wgbf = dt("wgbf", [D, D], BF16, kind="Internal").ap()
    wobf = dt("wobf", [D, D], BF16, kind="Internal").ap()
    b1bf = dt("b1bf", [NE, FF], BF16, kind="Internal").ap()
    b2bf = dt("b2bf", [NE, D], BF16, kind="Internal").ap()
    w1bf = dt("w1bf", [NE, FF, D], BF16, kind="Internal").ap()
    w2bf = dt("w2bf", [NE, D, FF], BF16, kind="Internal").ap()
    w1t = dt("dbg_w1t" if debug else "w1t", [NE, D, FF], BF16,
             kind="ExternalOutput" if debug else "Internal").ap()
    w2t = dt("dbg_w2t" if debug else "w2t", [NE, FF, D], BF16,
             kind="ExternalOutput" if debug else "Internal").ap()
    hfp = dt("dbg_h" if debug else "hfp", [TOKS, D], F32,
             kind="ExternalOutput" if debug else "Internal").ap()
    hbf = dt("hbf", [TOKS, D], BF16, kind="Internal").ap()
    if debug:
        dbg_vt = dt("dbg_vt", [128, NCB * WAVE], F32, kind="ExternalOutput").ap()
        dbg_y = dt("dbg_y", [128, NCB * WAVE], F32, kind="ExternalOutput").ap()
        dbg_z = dt("dbg_z", [128, NCB * WAVE], BF16, kind="ExternalOutput").ap()
        dbg_lamb = dt("dbg_lamb", [128, NCB * WAVE], F32, kind="ExternalOutput").ap()
        dbg_xf = dt("dbg_xf", [128, TPW * D], F32, kind="ExternalOutput").ap()
        dbg_sm = dt("dbg_sm", [128, 16], F32, kind="ExternalOutput").ap()
        dbg_wn = dt("dbg_wn", [128, TPW * 16], F32, kind="ExternalOutput").ap()
        dbg_hid = dt("dbg_hid", [128, NFT * WAVE], BF16, kind="ExternalOutput").ap()
        dbg_acc = dt("dbg_acc", [128, TPW * D], F32, kind="ExternalOutput").ap()
        dbg_hfw = dt("dbg_hfw", [128, TPW * D], F32, kind="ExternalOutput").ap()
        dbg_ht = dt("dbg_ht", [128, 2 * NDC * WAVE], BF16, kind="ExternalOutput").ap()

    prog = Prog()
    P = prog

    with ExitStack() as outer:
        ent = outer.enter_context
        qPE = Ctr(ent(nc.semaphore("qPE")), 1)
        qDV = Ctr(ent(nc.semaphore("qDV")), 1)
        qSC = Ctr(ent(nc.semaphore("qSC")), 1)
        qSY = Ctr(ent(nc.semaphore("qSY")), 16)
        qAC = Ctr(ent(nc.semaphore("qAC")), 16)
        qGP = Ctr(ent(nc.semaphore("qGP")), 16)
        qSM = Ctr(ent(nc.semaphore("qSM")), 16)
        qCA = Ctr(ent(nc.semaphore("qCA")), 16)
        qCW1 = [Ctr(ent(nc.semaphore(f"qCWa{i}")), 16) for i in range(NE)]
        qCW2 = [Ctr(ent(nc.semaphore(f"qCWb{i}")), 16) for i in range(NE)]
        qB2 = Ctr(ent(nc.semaphore("qB2")), 16)
        qWT = Ctr(ent(nc.semaphore("qWT")), 16)
        qXT = [Ctr(ent(nc.semaphore("qXT0")), 16), Ctr(ent(nc.semaphore("qXT1")), 16)]
        qHT = [Ctr(ent(nc.semaphore("qHT0")), 16), Ctr(ent(nc.semaphore("qHT1")), 16)]
        qW1 = [Ctr(ent(nc.semaphore("qW1a")), 16), Ctr(ent(nc.semaphore("qW1b")), 16)]
        qXF = Ctr(ent(nc.semaphore("qXF")), 16)
        qHF = Ctr(ent(nc.semaphore("qHF")), 16)
        qHS = [Ctr(ent(nc.semaphore("qHS0")), 16), Ctr(ent(nc.semaphore("qHS1")), 16)]
        qBX = [Ctr(ent(nc.semaphore(f"qBX{i}")), 16) for i in range(4)]
        qBS = [Ctr(ent(nc.semaphore(f"qBS{i}")), 16) for i in range(4)]
        qw2g = [Ctr(ent(nc.semaphore(f"qw2g{i}")), 16) for i in range(4)]
        qOUT = [Ctr(ent(nc.semaphore("qOUT0")), 16), Ctr(ent(nc.semaphore("qOUT1")), 16)]

        ps = [ent(nc.psum_tensor(f"ps{i}", [128, 512], F32)) for i in range(8)]

        sb = nc.sbuf_tensor
        idbf = ent(sb("idbf", [128, 128], BF16))
        idf = ent(sb("idf", [16, 16], F32))
        onesrow = ent(sb("onesrow", [1, 128], F32))
        onescbf = ent(sb("onescbf", [128, 1], BF16))
        ones512 = ent(sb("ones512", [128, 512], F32))
        sel_sb = ent(sb("sel_sb", [NH, 128], F32))
        mask_sb = ent(sb("mask_sb", [NH, NCB], F32))
        lam_col = ent(sb("lam_col", [NH, 1], F32))
        lam_b = ent(sb("lam_b", [128, NCB], F32))
        gamc = ent(sb("gamc", [128, NCB], F32))
        betc = ent(sb("betc", [128, NCB], F32))
        rt_sb = ent(sb("rt_sb", [128, NDC * 16], BF16))
        rbq_bc = ent(sb("rbq_bc", [128, 16], F32))
        b2_sb = ent(sb("b2_sb", [NE, D], BF16))
        b1c = ent(sb("b1c", [128, NE * NFT], F32))
        smallA = ent(sb("smallA", [NH, D], F32))
        smallB = ent(sb("smallB", [NH, D], F32))
        smallC = ent(sb("smallC", [NH, D], F32))
        smallBf = ent(sb("smallBf", [NH, D], BF16))
        scol = ent(sb("scol", [NH, 2], F32))
        rrow = ent(sb("rrow", [1, 16], F32))
        b1cb = ent(sb("b1cb", [128, NE * NFT], BF16))
        bnc = ent(sb("bnc", [128, 4 * 1024], BF16))

        def vchain(build):
            v = P.op("vector", build, inc=qDV)
            P.wait("vector", qDV, v)
            return v

        for o, i in (
            (idbf[:, :], c_idbf[:, :]), (idf[:, :], c_idf[:, :]),
            (onesrow[:, :], c_onesrow[:, :]), (onescbf[:, :], c_onescbf[:, :]),
            (ones512[:, :], c_ones512[:, :]),
            (sel_sb[:, :], c_sel[:, :]), (mask_sb[:, :], c_mask[:, :]),
            (smallA[:, :], keys[:, :]), (smallB[:, :], rwt[:, :]),
            (scol[:, 0:1], rb[:, :]), (lam_col[:, :], beta[:, :]),
        ):
            P.op("gpsimd", lambda E, o=o, i=i: E.dma_start(out=o, in_=i), inc=qSM)
        for cb in range(NCB):
            P.op("gpsimd", lambda E, cb=cb: E.dma_start(
                out=gamc[:, cb:cb + 1], in_=gam[cb * 128:(cb + 1) * 128, :]), inc=qSM)
            P.op("gpsimd", lambda E, cb=cb: E.dma_start(
                out=betc[:, cb:cb + 1], in_=bet[cb * 128:(cb + 1) * 128, :]), inc=qSM)
        gp_small = qSM.n

        P.op("gpsimd", lambda E: E.dma_start(out=xbf[:, :], in_=x[:, :]), inc=qCA)
        for o, i in ((wqbf, wq), (wvbf, wv), (wgbf, wg), (wobf, wo)):
            P.op("gpsimd", lambda E, o=o, i=i: E.dma_start(out=o[:, :], in_=i[:, :]), inc=qCA)
        P.op("gpsimd", lambda E: E.dma_start(out=b1bf[:, :], in_=b1[:, :]), inc=qCA)
        P.op("gpsimd", lambda E: E.dma_start(out=b2bf[:, :], in_=b2[:, :]), inc=qCA)
        gp_cast_all = qCA.n
        P.wait("gpsimd", qCA, gp_cast_all)
        P.op("gpsimd", lambda E: E.dma_start(out=b2_sb[:, :], in_=b2bf[:, :]), inc=qB2)
        b2_done = qB2.n

        cast_emitted = set()

        def emit_cast(e_):
            if e_ in cast_emitted or e_ >= NE:
                return
            cast_emitted.add(e_)
            P.op("gpsimd", lambda E, e_=e_: E.dma_start(out=w1bf[e_, :, :], in_=w1[e_, :, :]), inc=qCW1[e_])
            P.op("gpsimd", lambda E, e_=e_: E.dma_start(out=w2bf[e_, :, :], in_=w2[e_, :, :]), inc=qCW2[e_])

        P.wait("scalar", qSM, gp_small)
        P.op("scalar", lambda E: E.activation(lam_col[:, :], lam_col[:, :], AF.Sigmoid), inc=qSC)
        sc_lam = qSC.n
        P.wait("vector", qSC, sc_lam)
        P.op("vector", lambda E: E.tensor_scalar_mul(mask_sb[:, :], mask_sb[:, :], lam_col[:, :]), inc=qDV)
        dv_x2 = qDV.n
        P.wait("tensor", qDV, dv_x2)
        P.op("tensor", lambda E: E.matmul(ps[6][0:128, 0:NCB], sel_sb[:, :], mask_sb[:, :],
                                          start=True, stop=True), inc=qPE)
        pe_lamb = qPE.n
        P.wait("vector", qPE, pe_lamb)
        P.op("vector", lambda E: E.tensor_copy(lam_b[:, :], ps[6][0:128, 0:NCB]), inc=qDV)
        dv_lamb = qDV.n

        P.wait("vector", qSM, gp_small)
        P.op("vector", lambda E: E.scalar_tensor_tensor(
            smallC[:, :], smallA[:, :], 2.0, smallB[:, :], ALU.mult, ALU.add))
        P.op("vector", lambda E: E.tensor_copy(smallBf[:, :], smallC[:, :]), inc=qDV)
        dv_rbf = qDV.n
        vchain(lambda E: E.tensor_mul(smallB[:, :], smallA[:, :], smallA[:, :]))
        vchain(lambda E: E.reduce_sum(scol[:, 1:2], smallB[:, :], axis=AX.X))
        dv_rbq = vchain(lambda E: E.tensor_sub(scol[:, 1:2], scol[:, 0:1], scol[:, 1:2]))
        P.wait("tensor", qDV, dv_rbf)
        prev_evac = 0
        for dc in range(NDC):
            P.wait("tensor", qDV, prev_evac)
            P.op("tensor", lambda E, dc=dc: E.matmul(
                ps[7][0:128, 0:16], smallBf[0:16, dc * 128:(dc + 1) * 128],
                idbf[0:16, 0:16], start=True, stop=True), inc=qPE)
            v = qPE.n
            P.wait("vector", qPE, v)
            prev_evac = P.op("vector", lambda E, dc=dc: E.tensor_copy(
                rt_sb[:, dc * 16:(dc + 1) * 16], ps[7][0:128, 0:16]), inc=qDV)
        dv_rt = prev_evac
        P.wait("tensor", qDV, dv_rbq)
        P.wait("tensor", qDV, dv_rt)
        P.op("tensor", lambda E: E.matmul(ps[7][0:1, 0:16], scol[0:16, 1:2], idf[0:16, 0:16],
                                          start=True, stop=True), inc=qPE)
        v = qPE.n
        P.wait("vector", qPE, v)
        P.op("vector", lambda E: E.tensor_copy(rrow[:, :], ps[7][0:1, 0:16]), inc=qDV)
        dv_rrow = qDV.n
        P.wait("tensor", qDV, dv_rrow)
        P.op("tensor", lambda E: E.matmul(ps[7][0:128, 0:16], onesrow[0:1, :], rrow[0:1, :],
                                          start=True, stop=True), inc=qPE)
        v = qPE.n
        P.wait("vector", qPE, v)
        P.op("vector", lambda E: E.tensor_copy(rbq_bc[:, :], ps[7][0:128, 0:16]), inc=qDV)
        dv_rbqbc = qDV.n

        P.wait("sync", qCA, gp_cast_all)
        b1v = b1bf.rearrange("e (r p) -> (e r) p", p=128)
        P.op("sync", lambda E: E.dma_start(out=b1cb[:, :], in_=b1v[:, :], transpose=True), inc=qWT)

        h_store_snap = [[0] * TPW for _ in range(NWAVES)]
        sc_h_snap = [[0] * TPW for _ in range(NWAVES)]

        spill_slot = [0]
        spill_started = set()
        spill_done = {}

        def spill_chunks_of(e_):
            ch = [("w1", e_, dc, fq) for dc in range(NDC) for fq in range(4)]
            ch += [("w2", e_, fc, 0) for fc in range(NFT)]
            return ch

        def emit_spill_chunk(chunk):
            kind, e_, i1, i2 = chunk
            slot = spill_slot[0] % 4
            spill_slot[0] += 1
            if (e_, kind) not in spill_started:
                spill_started.add((e_, kind))
                P.wait("sync", (qCW1 if kind == "w1" else qCW2)[e_], 16)
            P.wait("sync", qBS[slot], qBS[slot].n)
            if kind == "w1":
                dc, fq = i1, i2
                sap = w1bf[e_, fq * 1024:(fq + 1) * 1024, dc * 128:(dc + 1) * 128]
                dap = w1t[e_, dc * 128:(dc + 1) * 128, fq * 1024:(fq + 1) * 1024]
            else:
                fc = i1
                sap = w2bf[e_, :, fc * 128:(fc + 1) * 128]
                dap = w2t[e_, fc * 128:(fc + 1) * 128, :]
            P.op("sync", lambda E, sap=sap, slot=slot: E.dma_start(
                out=bnc[:, slot * 1024:(slot + 1) * 1024], in_=sap,
                transpose=True), inc=qBX[slot])
            P.wait("scalar", qBX[slot], qBX[slot].n)
            P.op("scalar", lambda E, dap=dap, slot=slot: E.dma_start(
                out=dap, in_=bnc[:, slot * 1024:(slot + 1) * 1024]), inc=qBS[slot])
            spill_done[e_] = [(qBS[s], qBS[s].n) for s in range(4)]

        with ExitStack() as attn_ctx:
            aent = attn_ctx.enter_context
            wqT = aent(sb("wqT", [128, NDC * D], BF16))
            wvT = aent(sb("wvT", [128, NDC * D], BF16))
            wgT = aent(sb("wgT", [128, NDC * D], BF16))
            woT = aent(sb("woT", [128, NDC * D], BF16))
            xt_b = aent(sb("xt_b", [128, 2 * NDC * WAVE], BF16))
            xf_b = aent(sb("xf_b", [128, TPW * D], F32))
            vT = aent(sb("vT", [128, NCB * WAVE], F32))
            Yb = aent(sb("Yb", [128, NCB * WAVE], F32))
            zb = aent(sb("zb", [128, NCB * WAVE], BF16))
            lamB = aent(sb("lamB", [128, NCB * WAVE], F32))
            ybfs = aent(sb("ybfs", [128, WAVE], BF16))
            y2fs = aent(sb("y2fs", [128, WAVE], BF16))
            gsil = aent(sb("gsil", [128, WAVE], F32))
            t1s = aent(sb("t1s", [128, WAVE], F32))
            hb_s = aent(sb("hb_s", [128, 2 * D], F32))
            carry = aent(sb("carry", [128, NCB], F32))
            murow = aent(sb("murow", [1, WAVE], F32))
            msrow = aent(sb("msrow", [1, WAVE], F32))
            varrow = aent(sb("varrow", [1, WAVE], F32))
            rstdrow = aent(sb("rstdrow", [1, WAVE], F32))
            s1c = aent(sb("s1c", [128, 8], F32))

            P.wait("sync", qCA, gp_cast_all)
            for wtn, srcw in ((wqT, wqbf), (wvT, wvbf), (wgT, wgbf), (woT, wobf)):
                for dc in range(NDC):
                    P.op("sync", lambda E, wtn=wtn, srcw=srcw, dc=dc: E.dma_start(
                        out=wtn[:, dc * D:(dc + 1) * D],
                        in_=srcw[:, dc * 128:(dc + 1) * 128], transpose=True), inc=qWT)
            sy_wt = qWT.n
            P.wait("vector", qWT, sy_wt)
            P.op("vector", lambda E: E.tensor_copy(b1c[:, :], b1cb[:, :]), inc=qDV)
            P.wait("scalar", qDV, dv_lamb)
            for cb in range(NCB):
                P.op("scalar", lambda E, cb=cb: E.mul(
                    lamB[:, cb * WAVE:(cb + 1) * WAVE], ones512[:, :],
                    lam_b[:, cb:cb + 1]), inc=qSC)
            sc_lamB = qSC.n

            n_attn_spill_exp = max(0, min(n_waves - 1, n_experts))

            projb_cons = [0, 0]
            projb_conseng = [None, None]
            aob_cons = [0, 0, 0, 0]
            st_cons = 0
            xt_done_pe = [0, 0]
            xf_done_dv = 0
            ybf_cons = [0, 0]
            gsil_cons = [0, 0]
            hb_cons = [0, 0]
            ao_end_pe = 0
            sc_h_prev_wave = 0
            carry_snap = [0] * NCB
            dXT = [0] * NWAVES
            dXF = [0] * NWAVES

            for w in range(n_waves):
                par = w % 2
                if w == 0:
                    P.wait("sync", qCA, gp_cast_all)
                P.wait("sync", qPE, xt_done_pe[par])
                for dc in range(NDC):
                    P.op("sync", lambda E, w=w, dc=dc, par=par: E.dma_start(
                        out=xt_b[:, (par * NDC + dc) * WAVE:(par * NDC + dc + 1) * WAVE],
                        in_=xbf[w * WAVE:(w + 1) * WAVE, dc * 128:(dc + 1) * 128],
                        transpose=True), inc=qXT[par])
                dXT[w] = qXT[par].n
                P.wait("gpsimd", qDV, xf_done_dv)
                xv = x[w * WAVE:(w + 1) * WAVE, :].rearrange("(t p) d -> p t d", p=128)
                P.op("gpsimd", lambda E, xv=xv, xf_b=xf_b: E.dma_start(
                    out=xf_b[:, :].rearrange("p (t d) -> p t d", t=TPW), in_=xv), inc=qXF)
                dXF[w] = qXF.n

                if w < n_experts:
                    emit_cast(w)
                if 0 < w <= n_attn_spill_exp:
                    for chunk in spill_chunks_of(w - 1):
                        emit_spill_chunk(chunk)

                P.wait("tensor", qXT[par], dXT[w])
                if w == 0:
                    P.wait("tensor", qWT, sy_wt)

                def proj_group(cb, wslab, bank, par=par):
                    if projb_conseng[bank] == "dv":
                        P.wait("tensor", qDV, projb_cons[bank])
                    elif projb_conseng[bank] == "sc":
                        P.wait("tensor", qSC, projb_cons[bank])
                    for dc in range(NDC):
                        P.op("tensor", lambda E, cb=cb, dc=dc, wslab=wslab, bank=bank, par=par:
                             E.matmul(ps[bank][:, 0:WAVE],
                                      wslab[:, dc * D + cb * 128: dc * D + (cb + 1) * 128],
                                      xt_b[:, (par * NDC + dc) * WAVE:(par * NDC + dc + 1) * WAVE],
                                      start=(dc == 0), stop=(dc == NDC - 1)),
                             inc=(qPE if dc == NDC - 1 else None))
                    return qPE.n

                for cb in range(NCB):
                    bank = cb % 2
                    g = proj_group(cb, wvT, bank)
                    P.wait("vector", qPE, g)
                    projb_cons[bank] = P.op("vector", lambda E, cb=cb, bank=bank: E.tensor_copy(
                        vT[:, cb * WAVE:(cb + 1) * WAVE], ps[bank][:, 0:WAVE]), inc=qDV)
                    projb_conseng[bank] = "dv"

                if w == 0:
                    P.wait("vector", qSC, sc_lamB)
                else:
                    P.wait("vector", qSC, sc_h_prev_wave)
                for cb in range(NCB):
                    init = carry[:, cb:cb + 1] if (w % 4) != 0 else 0.0
                    if (w % 4) != 0:
                        P.wait("vector", qDV, carry_snap[cb])
                    vchain(lambda E, cb=cb, init=init: E.tensor_tensor_scan(
                        Yb[:, cb * WAVE:(cb + 1) * WAVE],
                        lamB[:, cb * WAVE:(cb + 1) * WAVE],
                        vT[:, cb * WAVE:(cb + 1) * WAVE],
                        init, ALU.mult, ALU.add))
                    if (w % 4) != 3:
                        carry_snap[cb] = P.op("vector", lambda E, cb=cb: E.tensor_copy(
                            carry[:, cb:cb + 1],
                            Yb[:, (cb + 1) * WAVE - 1:(cb + 1) * WAVE]), inc=qDV)

                P.wait("tensor", qDV, dv_lamb)
                P.wait("tensor", qDV, dv_rbqbc)
                if w > 0:
                    P.wait("tensor", qDV, st_cons)
                for cb in range(NCB):
                    bank = cb % 2
                    sp = 0
                    g = proj_group(cb, wqT, bank)
                    P.wait("vector", qPE, g)
                    P.op("vector", lambda E, cb=cb, bank=bank: E.tensor_mul(
                        Yb[:, cb * WAVE:(cb + 1) * WAVE],
                        Yb[:, cb * WAVE:(cb + 1) * WAVE], ps[bank][:, 0:WAVE]), inc=qDV)
                    projb_cons[bank] = qDV.n
                    projb_conseng[bank] = "dv"
                    dv_y = qDV.n
                    P.wait("vector", qPE, ybf_cons[sp])
                    P.op("vector", lambda E, cb=cb, sp=sp: E.tensor_copy(
                        ybfs[:, sp * WAVE:(sp + 1) * WAVE],
                        Yb[:, cb * WAVE:(cb + 1) * WAVE]), inc=qDV)
                    dv_ybf = qDV.n
                    P.wait("scalar", qPE, ybf_cons[sp])
                    P.wait("scalar", qDV, dv_y)
                    P.op("scalar", lambda E, cb=cb, sp=sp: E.activation(
                        y2fs[:, sp * WAVE:(sp + 1) * WAVE],
                        Yb[:, cb * WAVE:(cb + 1) * WAVE], AF.Square), inc=qSC)
                    sc_y2 = qSC.n
                    P.wait("tensor", qDV, dv_ybf)
                    P.wait("tensor", qSC, sc_y2)
                    P.op("tensor", lambda E, cb=cb, sp=sp: E.matmul(
                        ps[6][0:1, 0:WAVE], onescbf[:, :],
                        ybfs[:, sp * WAVE:(sp + 1) * WAVE],
                        start=(cb == 0), stop=(cb == NCB - 1)))
                    ybf_cons[sp] = P.op("tensor", lambda E, cb=cb, sp=sp: E.matmul(
                        ps[7][0:1, 0:WAVE], onescbf[:, :],
                        y2fs[:, sp * WAVE:(sp + 1) * WAVE],
                        start=(cb == 0), stop=(cb == NCB - 1)), inc=qPE)
                pe_stats = qPE.n

                P.wait("vector", qPE, pe_stats)
                vchain(lambda E: E.tensor_scalar_mul(murow[:, :], ps[6][0:1, 0:WAVE], 1.0 / D))
                vchain(lambda E: E.tensor_scalar_mul(msrow[:, :], ps[7][0:1, 0:WAVE], 1.0 / D))
                vchain(lambda E: E.tensor_mul(varrow[:, :], murow[:, :], murow[:, :]))
                vchain(lambda E: E.tensor_sub(varrow[:, :], msrow[:, :], varrow[:, :]))
                dv_var = vchain(lambda E: E.tensor_scalar_add(varrow[:, :], varrow[:, :], EPS))
                P.wait("scalar", qDV, dv_var)
                P.op("scalar", lambda E: E.activation(rstdrow[:, :], varrow[:, :], AF.Ln))
                P.op("scalar", lambda E: E.activation(rstdrow[:, :], rstdrow[:, :], AF.Exp, scale=-0.5), inc=qSC)
                sc_rstd = qSC.n
                P.wait("tensor", qDV, dv_var)
                P.wait("tensor", qSC, sc_rstd)
                P.op("tensor", lambda E: E.matmul(ps[6][0:128, 0:WAVE], onesrow[0:1, :],
                                                  murow[0:1, :], start=True, stop=True))
                P.op("tensor", lambda E: E.matmul(ps[7][0:128, 0:WAVE], onesrow[0:1, :],
                                                  rstdrow[0:1, :], start=True, stop=True), inc=qPE)
                pe_bc = qPE.n

                if w > 0:
                    P.wait("vector", qPE, ao_end_pe)
                for cb in range(NCB):
                    bank = cb % 2
                    sp = 0
                    g = proj_group(cb, wgT, bank)
                    P.wait("scalar", qPE, g)
                    P.wait("scalar", qDV, gsil_cons[sp])
                    projb_cons[bank] = P.op("scalar", lambda E, sp=sp, bank=bank: E.activation(
                        gsil[:, sp * WAVE:(sp + 1) * WAVE], ps[bank][:, 0:WAVE], AF.Silu), inc=qSC)
                    projb_conseng[bank] = "sc"
                    sc_sil = qSC.n
                    P.wait("vector", qPE, pe_bc)
                    P.wait("vector", qSC, sc_sil)
                    P.op("vector", lambda E, cb=cb: E.tensor_sub(
                        t1s[:, :], Yb[:, cb * WAVE:(cb + 1) * WAVE], ps[6][0:128, 0:WAVE]))
                    P.op("vector", lambda E: E.tensor_mul(t1s[:, :], t1s[:, :], ps[7][0:128, 0:WAVE]))
                    P.op("vector", lambda E, cb=cb: E.tensor_scalar(
                        t1s[:, :], t1s[:, :], gamc[:, cb:cb + 1], betc[:, cb:cb + 1],
                        ALU.mult, ALU.add))
                    gsil_cons[sp] = P.op("vector", lambda E, cb=cb, sp=sp: E.tensor_mul(
                        zb[:, cb * WAVE:(cb + 1) * WAVE], t1s[:, :],
                        gsil[:, sp * WAVE:(sp + 1) * WAVE]), inc=qDV)
                z_done_dv = qDV.n
                st_cons = z_done_dv

                P.wait("tensor", qDV, z_done_dv)
                for tt in range(TPW):
                    bpair = (2 + 2 * (tt % 2), 3 + 2 * (tt % 2))
                    P.wait("tensor", qDV, aob_cons[bpair[0] - 2])
                    P.wait("tensor", qDV, aob_cons[bpair[1] - 2])
                    for cb in range(NCB):
                        for h in range(2):
                            P.op("tensor", lambda E, cb=cb, tt=tt, h=h, b=bpair[h]:
                                 E.matmul(ps[b][:, 0:WAVE],
                                          zb[:, cb * WAVE + tt * 128: cb * WAVE + (tt + 1) * 128],
                                          woT[:, cb * D + h * WAVE: cb * D + (h + 1) * WAVE],
                                          start=(cb == 0), stop=(cb == NCB - 1)),
                                 inc=(qPE if (cb == NCB - 1 and h == 1) else None))
                    pe_ao = qPE.n
                    P.wait("vector", qPE, pe_ao)
                    P.wait("vector", qXF, dXF[w])
                    if tt > 0:
                        P.wait("vector", qSC, sc_h_snap[w][tt - 1])
                    P.op("vector", lambda E, tt=tt, b=bpair[0]: E.tensor_add(
                        Yb[:, 0:WAVE], xf_b[:, tt * D: tt * D + WAVE], ps[b][:, 0:WAVE]))
                    P.op("vector", lambda E, tt=tt, b=bpair[1]: E.tensor_add(
                        Yb[:, WAVE:D], xf_b[:, tt * D + WAVE: (tt + 1) * D], ps[b][:, 0:WAVE]), inc=qDV)
                    aob_cons[bpair[0] - 2] = qDV.n
                    aob_cons[bpair[1] - 2] = qDV.n
                    vchain(lambda E: E.reduce_sum(s1c[:, 0:1], Yb[:, 0:D], axis=AX.X))
                    vchain(lambda E: E.tensor_mul(Yb[:, D:2 * D], Yb[:, 0:D], Yb[:, 0:D]))
                    vchain(lambda E: E.reduce_sum(s1c[:, 1:2], Yb[:, D:2 * D], axis=AX.X))
                    vchain(lambda E: E.tensor_scalar_mul(s1c[:, 2:3], s1c[:, 0:1], 1.0 / D))
                    vchain(lambda E: E.tensor_scalar_mul(s1c[:, 3:4], s1c[:, 1:2], 1.0 / D))
                    vchain(lambda E: E.tensor_mul(s1c[:, 4:5], s1c[:, 2:3], s1c[:, 2:3]))
                    vchain(lambda E: E.tensor_sub(s1c[:, 4:5], s1c[:, 3:4], s1c[:, 4:5]))
                    dv_ln1 = vchain(lambda E: E.tensor_scalar_add(s1c[:, 4:5], s1c[:, 4:5], EPS))
                    P.wait("scalar", qDV, dv_ln1)
                    P.op("scalar", lambda E: E.activation(s1c[:, 5:6], s1c[:, 4:5], AF.Ln))
                    P.op("scalar", lambda E: E.activation(s1c[:, 5:6], s1c[:, 5:6], AF.Exp, scale=-0.5), inc=qSC)
                    sc_r = qSC.n
                    P.wait("vector", qSC, sc_r)
                    vchain(lambda E: E.tensor_mul(s1c[:, 6:7], s1c[:, 2:3], s1c[:, 5:6]))
                    dv_nb = vchain(lambda E: E.tensor_scalar_mul(s1c[:, 6:7], s1c[:, 6:7], -1.0))
                    hpar = tt % 2
                    P.wait("scalar", qDV, dv_nb)
                    P.wait("scalar", qHS[hpar], hb_cons[hpar])
                    sc_h_snap[w][tt] = P.op("scalar", lambda E, hpar=hpar: E.activation(
                        hb_s[:, hpar * D:(hpar + 1) * D], Yb[:, 0:D], AF.Identity,
                        bias=s1c[:, 6:7], scale=s1c[:, 5:6]), inc=qSC)
                    P.wait("gpsimd", qSC, sc_h_snap[w][tt])
                    r0 = w * WAVE + tt * 128
                    P.op("gpsimd", lambda E, r0=r0, hpar=hpar: E.dma_start(
                        out=hfp[r0:r0 + 128, :], in_=hb_s[:, hpar * D:(hpar + 1) * D]), inc=qHS[hpar])
                    P.op("gpsimd", lambda E, r0=r0, hpar=hpar: E.dma_start(
                        out=hbf[r0:r0 + 128, :], in_=hb_s[:, hpar * D:(hpar + 1) * D]), inc=qHS[hpar])
                    hb_cons[hpar] = qHS[hpar].n
                    h_store_snap[w][tt] = (qHS[0].n, qHS[1].n)
                ao_end_pe = qPE.n
                sc_h_prev_wave = sc_h_snap[w][TPW - 1]
                xt_done_pe[par] = qPE.n
                xf_done_dv = qDV.n

            if debug:
                P.wait("gpsimd", qDV, qDV.n)
                P.wait("gpsimd", qSC, qSC.n)
                P.wait("gpsimd", qPE, qPE.n)
                for o, i in ((dbg_vt, vT), (dbg_y, Yb), (dbg_z, zb),
                             (dbg_lamb, lamB), (dbg_xf, xf_b)):
                    P.op("gpsimd", lambda E, o=o, i=i: E.dma_start(out=o[:, :], in_=i[:, :]), inc=qGP)
                P.op("gpsimd", lambda E: E.dma_start(out=dbg_sm[:, 0:NCB], in_=lam_b[:, :]), inc=qGP)
                P.op("gpsimd", lambda E: E.dma_start(out=dbg_sm[:, NCB:2 * NCB], in_=gamc[:, :]), inc=qGP)

            attn_end = [(qPE, qPE.n), (qDV, qDV.n), (qSC, qSC.n),
                        (qSY, qSY.n), (qGP, qGP.n), (qSM, qSM.n), (qCA, qCA.n),
                        (qB2, qB2.n), (qWT, qWT.n), (qXT[0], qXT[0].n),
                        (qXT[1], qXT[1].n), (qXF, qXF.n),
                        (qHS[0], qHS[0].n), (qHS[1], qHS[1].n)]

        with ExitStack() as ffn_ctx:
            fent = ffn_ctx.enter_context
            ht_b = fent(sb("ht_b", [128, 2 * NDC * WAVE], BF16))
            w1g = fent(sb("w1g", [128, 2 * NDC * 1024], BF16))
            w2r = fent(sb("w2r", [128, 32 * WAVE], BF16))
            hid = fent(sb("hid", [128, NFT * WAVE], BF16))
            accb = fent(sb("accb", [128, TPW * D], F32))
            hfw = fent(sb("hfw", [128, TPW * D], F32))
            outs = fent(sb("outs", [128, 2 * D], F32))
            wnw = fent(sb("wnw", [128, TPW * 16], F32))
            wnbf = fent(sb("wnbf", [128, TPW * 16], BF16))
            wnT = fent(sb("wnT", [16, TPW * 128], BF16))
            lgs = fent(sb("lgs", [128, 16], F32))
            l2s = fent(sb("l2s", [128, 16], F32))
            iss = fent(sb("iss", [128, 16], F32))
            exs = fent(sb("exs", [128, 16], F32))
            mcol = fent(sb("mcol", [128, 8], F32))
            e1c = fent(sb("e1c", [128, 8], F32))

            for eng in Prog.ENGS:
                for ctr_, val_ in attn_end:
                    P.wait(eng, ctr_, val_)

            ht_done_pe = [0, 0]
            hfw_done_dv = 0
            outs_done = [0, 0]
            w1g_done_pe = [0, 0]
            w2grp_done_pe = [0] * 4
            f1_done_sc = [0, 0]
            f2_done_dv = [0, 0, 0, 0]
            lg_done_dv = 0
            wn_state = [0, 0]
            dv_wnbf_l = [0] * TPW
            wnbf_guard_pe = 0
            sc_ex_prev = 0
            ep_act_sc = [0] * TPW

            for w in range(n_waves):
                par = w % 2
                P.wait("sync", qPE, ht_done_pe[par])
                for dc in range(NDC):
                    P.op("sync", lambda E, w=w, dc=dc, par=par: E.dma_start(
                        out=ht_b[:, (par * NDC + dc) * WAVE:(par * NDC + dc + 1) * WAVE],
                        in_=hbf[w * WAVE:(w + 1) * WAVE, dc * 128:(dc + 1) * 128],
                        transpose=True), inc=qHT[par])
                dHT = qHT[par].n
                P.wait("gpsimd", qDV, hfw_done_dv)
                hv = hfp[w * WAVE:(w + 1) * WAVE, :].rearrange("(t p) d -> p t d", p=128)
                P.op("gpsimd", lambda E, hv=hv, hfw=hfw: E.dma_start(
                    out=hfw[:, :].rearrange("p (t d) -> p t d", t=TPW), in_=hv), inc=qHF)
                dHF = qHF.n

                P.wait("tensor", qHT[par], dHT)
                for tt in range(TPW):
                    P.wait("tensor", qDV, lg_done_dv)
                    for dc in range(NDC):
                        P.op("tensor", lambda E, dc=dc, tt=tt, par=par: E.matmul(
                            ps[6][:, 0:16],
                            ht_b[:, (par * NDC + dc) * WAVE + tt * 128:
                                 (par * NDC + dc) * WAVE + (tt + 1) * 128],
                            rt_sb[:, dc * 16:(dc + 1) * 16],
                            start=(dc == 0), stop=(dc == NDC - 1)),
                            inc=(qPE if dc == NDC - 1 else None))
                    pe_lg = qPE.n
                    P.wait("vector", qPE, pe_lg)
                    P.wait("vector", qSC, sc_ex_prev)
                    lg_done_dv = vchain(lambda E: E.tensor_add(lgs[:, :], ps[6][:, 0:16], rbq_bc[:, :]))
                    vchain(lambda E: E.reduce_max(mcol[:, 0:1], lgs[:, :], axis=AX.X))
                    vchain(lambda E: E.tensor_single_scalar(iss[:, :], lgs[:, :], mcol[:, 0:1], ALU.is_ge))
                    vchain(lambda E: E.scalar_tensor_tensor(l2s[:, :], iss[:, :], -BIG, lgs[:, :], ALU.mult, ALU.add))
                    for k in (1, 2):
                        vchain(lambda E, k=k: E.reduce_max(mcol[:, k:k + 1], l2s[:, :], axis=AX.X))
                        vchain(lambda E, k=k: E.tensor_single_scalar(iss[:, :], l2s[:, :], mcol[:, k:k + 1], ALU.is_ge))
                        vchain(lambda E: E.scalar_tensor_tensor(l2s[:, :], iss[:, :], -BIG, l2s[:, :], ALU.mult, ALU.add))
                    vchain(lambda E: E.reduce_max(mcol[:, 3:4], l2s[:, :], axis=AX.X))
                    vchain(lambda E: E.tensor_single_scalar(iss[:, :], lgs[:, :], mcol[:, 3:4], ALU.is_ge))
                    dv_top = vchain(lambda E: E.tensor_scalar_mul(mcol[:, 4:5], mcol[:, 0:1], -1.0))
                    P.wait("scalar", qDV, dv_top)
                    P.op("scalar", lambda E: E.activation(exs[:, :], lgs[:, :], AF.Exp, bias=mcol[:, 4:5]), inc=qSC)
                    sc_ex = qSC.n
                    sc_ex_prev = sc_ex
                    P.wait("vector", qSC, sc_ex)
                    vchain(lambda E: E.tensor_mul(exs[:, :], exs[:, :], iss[:, :]))
                    vchain(lambda E: E.reduce_sum(mcol[:, 5:6], exs[:, :], axis=AX.X))
                    vchain(lambda E: E.reciprocal(mcol[:, 6:7], mcol[:, 5:6]))
                    vchain(lambda E, tt=tt: E.tensor_scalar_mul(
                        wnw[:, tt * 16:(tt + 1) * 16], exs[:, :], mcol[:, 6:7]))
                    if tt == 0:
                        P.wait("vector", qPE, wnbf_guard_pe)
                    P.op("vector", lambda E, tt=tt: E.tensor_copy(
                        wnbf[:, tt * 16:(tt + 1) * 16],
                        wnw[:, tt * 16:(tt + 1) * 16]), inc=qDV)
                    dv_wnbf_l[tt] = qDV.n

                def emit_wninit(w=w, dHF=dHF, dv_wnbf_l2=None):
                    for tt in range(TPW):
                        P.wait("tensor", qDV, dv_wnbf_l[tt])
                        if wn_state[0]:
                            P.wait("tensor", qDV, wn_state[0])
                        P.op("tensor", lambda E, tt=tt: E.matmul(
                            ps[7][0:16, 0:128], wnbf[:, tt * 16:(tt + 1) * 16],
                            idbf[:, :], start=True, stop=True), inc=qPE)
                        pe_wnt = qPE.n
                        wn_state[1] = pe_wnt
                        P.wait("vector", qPE, pe_wnt)
                        P.op("vector", lambda E, tt=tt: E.tensor_copy(
                            wnT[0:16, tt * 128:(tt + 1) * 128], ps[7][0:16, 0:128]), inc=qDV)
                        wn_state[0] = qDV.n
                    P.wait("tensor", qDV, wn_state[0])
                    if w == 0:
                        P.wait("tensor", qB2, b2_done)
                    for tt in range(TPW):
                        bk = (2 + 2 * (tt % 2), 3 + 2 * (tt % 2))
                        P.wait("tensor", qDV, f2_done_dv[bk[0] - 2])
                        P.wait("tensor", qDV, f2_done_dv[bk[1] - 2])
                        for h in range(2):
                            P.op("tensor", lambda E, tt=tt, h=h, b=bk[h]: E.matmul(
                                ps[b][:, 0:WAVE], wnT[0:16, tt * 128:(tt + 1) * 128],
                                b2_sb[0:16, h * WAVE:(h + 1) * WAVE],
                                start=True, stop=True), inc=(qPE if h == 1 else None))
                        pe_bias = qPE.n
                        P.wait("vector", qPE, pe_bias)
                        P.wait("vector", qHF, dHF)
                        if ep_act_sc[tt]:
                            P.wait("vector", qSC, ep_act_sc[tt])
                        for h in range(2):
                            P.op("vector", lambda E, tt=tt, h=h, b=bk[h]: E.scalar_tensor_tensor(
                                accb[:, tt * D + h * WAVE: tt * D + (h + 1) * WAVE],
                                ps[b][:, 0:WAVE], 1.0,
                                hfw[:, tt * D + h * WAVE: tt * D + (h + 1) * WAVE],
                                ALU.mult, ALU.add), inc=qDV)
                        f2_done_dv[bk[0] - 2] = qDV.n
                        f2_done_dv[bk[1] - 2] = qDV.n

                groups_seq = [(e_, fg_) for e_ in range(n_experts) for fg_ in range(4)]
                dW1_snap = {}

                def emit_w1_load(gidx, w=w):
                    e_, fg_ = groups_seq[gidx]
                    gpar = gidx % 2
                    P.wait("sync", qPE, w1g_done_pe[gpar])
                    if w == 0:
                        if e_ not in spill_done:
                            for chunk in spill_chunks_of(e_):
                                emit_spill_chunk(chunk)
                        for ctr_, val_ in spill_done[e_]:
                            P.wait("sync", ctr_, val_)
                    P.op("sync", lambda E, e_=e_, fg_=fg_, gpar=gpar: E.dma_start(
                        out=w1g[:, gpar * NDC * 1024:(gpar + 1) * NDC * 1024]
                            .rearrange("p (j f) -> p j f", j=NDC),
                        in_=w1t[e_].rearrange("(j p) f -> p j f", p=128)
                            [:, :, fg_ * 1024:(fg_ + 1) * 1024]), inc=qW1[gpar])
                    dW1_snap[gidx] = (gpar, qW1[gpar].n)

                emit_w1_load(0)
                if len(groups_seq) > 1:
                    emit_w1_load(1)
                for e in range(n_experts):
                    dW2g = [[0] * 4, [0] * 4]
                    if w == 0:
                        if e not in spill_done:
                            for chunk in spill_chunks_of(e):
                                emit_spill_chunk(chunk)
                        for ctr_, val_ in spill_done[e]:
                            P.wait("scalar", ctr_, val_)
                    for g8 in range(4):
                        P.wait("scalar", qPE, w2grp_done_pe[g8])
                        dW2g[0][g8] = P.op("scalar", lambda E, e=e, g8=g8: E.dma_start(
                            out=w2r[:, g8 * 8 * WAVE:(g8 + 1) * 8 * WAVE]
                                .rearrange("p (j n) -> p j n", j=8),
                            in_=w2t[e].rearrange("(j p) d -> p j d", p=128)
                                [:, g8 * 8:(g8 + 1) * 8, 0:WAVE]), inc=qw2g[g8])

                    sc_gelu = [0] * NFT
                    for ft in range(NFT):
                        fg, ftl = divmod(ft, 8)
                        bank = ft % 2
                        gidx = e * 4 + fg
                        gpar = gidx % 2
                        if ftl == 0 and gidx in dW1_snap:
                            P.wait("tensor", qW1[dW1_snap[gidx][0]], dW1_snap[gidx][1])
                        if w == 0 and e == 0 and ft == 0:
                            P.wait("tensor", qHT[par], dHT)
                        P.wait("tensor", qSC, f1_done_sc[bank])
                        for dc in range(NDC):
                            P.op("tensor", lambda E, ftl=ftl, dc=dc, bank=bank, gpar=gpar, par=par:
                                 E.matmul(ps[bank][:, 0:WAVE],
                                          w1g[:, (gpar * NDC + dc) * 1024 + ftl * 128:
                                              (gpar * NDC + dc) * 1024 + (ftl + 1) * 128],
                                          ht_b[:, (par * NDC + dc) * WAVE:(par * NDC + dc + 1) * WAVE],
                                          start=(dc == 0), stop=(dc == NDC - 1)),
                                 inc=(qPE if dc == NDC - 1 else None))
                        pe_f1 = qPE.n
                        if w == 0:
                            se = n_attn_spill_exp + e
                            if se < n_experts:
                                ch = spill_chunks_of(se)
                                for chunk in ch[ft * 2:(ft + 1) * 2]:
                                    emit_spill_chunk(chunk)
                        if ftl == 7:
                            w1g_done_pe[gpar] = pe_f1
                            if gidx + 2 < len(groups_seq):
                                emit_w1_load(gidx + 2)
                            if w == 0 and fg == 0:
                                emit_cast(n_waves + e)
                        P.wait("scalar", qPE, pe_f1)
                        bcol = e * NFT + ft
                        f1_done_sc[bank] = P.op("scalar", lambda E, ft=ft, bank=bank, bcol=bcol:
                                                E.activation(hid[:, ft * WAVE:(ft + 1) * WAVE],
                                                             ps[bank][:, 0:WAVE], AF.Gelu,
                                                             bias=b1c[:, bcol:bcol + 1]), inc=qSC)
                        sc_gelu[ft] = f1_done_sc[bank]

                    if e == 0:
                        emit_wninit()
                        hfw_done_dv = qDV.n
                        wnbf_guard_pe = wn_state[1]

                    for dh in range(2):
                        for fc in range(32):
                            if fc % 8 == 0:
                                P.wait("tensor", qw2g[fc // 8], dW2g[dh][fc // 8])
                            P.wait("tensor", qSC, sc_gelu[fc])
                            for tt in range(TPW):
                                b = 2 + tt
                                if fc == 0:
                                    P.wait("tensor", qDV, f2_done_dv[tt])
                                P.op("tensor", lambda E, fc=fc, tt=tt, b=b:
                                     E.matmul(ps[b][:, 0:WAVE],
                                              hid[:, fc * WAVE + tt * 128: fc * WAVE + (tt + 1) * 128],
                                              w2r[:, fc * WAVE:(fc + 1) * WAVE],
                                              start=(fc == 0), stop=(fc == 31)),
                                     inc=(qPE if tt == TPW - 1 else None))
                            if fc % 8 == 7:
                                g8 = fc // 8
                                w2grp_done_pe[g8] = qPE.n
                                if dh == 0:
                                    P.wait("scalar", qPE, w2grp_done_pe[g8])
                                    dW2g[1][g8] = P.op("scalar", lambda E, e=e, g8=g8: E.dma_start(
                                        out=w2r[:, g8 * 8 * WAVE:(g8 + 1) * 8 * WAVE]
                                            .rearrange("p (j n) -> p j n", j=8),
                                        in_=w2t[e].rearrange("(j p) d -> p j d", p=128)
                                            [:, g8 * 8:(g8 + 1) * 8, WAVE:2 * WAVE]), inc=qw2g[g8])
                        pe_f2 = qPE.n
                        for tt in range(TPW):
                            b = 2 + tt
                            P.wait("vector", qPE, pe_f2)
                            P.op("vector", lambda E, e=e, dh=dh, tt=tt, b=b: E.scalar_tensor_tensor(
                                accb[:, tt * D + dh * WAVE: tt * D + (dh + 1) * WAVE],
                                ps[b][:, 0:WAVE], wnw[:, tt * 16 + e: tt * 16 + e + 1],
                                accb[:, tt * D + dh * WAVE: tt * D + (dh + 1) * WAVE],
                                ALU.mult, ALU.add), inc=qDV)
                            f2_done_dv[tt] = qDV.n

                for tt in range(TPW):
                    opar = tt % 2
                    P.wait("vector", qOUT[opar], outs_done[opar])
                    vchain(lambda E, tt=tt: E.reduce_sum(
                        e1c[:, 0:1], accb[:, tt * D:(tt + 1) * D], axis=AX.X))
                    vchain(lambda E, tt=tt, opar=opar: E.tensor_mul(
                        outs[:, opar * D:(opar + 1) * D],
                        accb[:, tt * D:(tt + 1) * D], accb[:, tt * D:(tt + 1) * D]))
                    vchain(lambda E, opar=opar: E.reduce_sum(
                        e1c[:, 1:2], outs[:, opar * D:(opar + 1) * D], axis=AX.X))
                    vchain(lambda E: E.tensor_scalar_mul(e1c[:, 2:3], e1c[:, 0:1], 1.0 / D))
                    vchain(lambda E: E.tensor_scalar_mul(e1c[:, 3:4], e1c[:, 1:2], 1.0 / D))
                    vchain(lambda E: E.tensor_mul(e1c[:, 4:5], e1c[:, 2:3], e1c[:, 2:3]))
                    vchain(lambda E: E.tensor_sub(e1c[:, 4:5], e1c[:, 3:4], e1c[:, 4:5]))
                    dv_ln2 = vchain(lambda E: E.tensor_scalar_add(e1c[:, 4:5], e1c[:, 4:5], EPS))
                    P.wait("scalar", qDV, dv_ln2)
                    P.op("scalar", lambda E: E.activation(e1c[:, 5:6], e1c[:, 4:5], AF.Ln))
                    P.op("scalar", lambda E: E.activation(e1c[:, 5:6], e1c[:, 5:6], AF.Exp, scale=-0.5), inc=qSC)
                    sc_r2 = qSC.n
                    P.wait("vector", qSC, sc_r2)
                    vchain(lambda E: E.tensor_mul(e1c[:, 6:7], e1c[:, 2:3], e1c[:, 5:6]))
                    dv_nb2 = vchain(lambda E: E.tensor_scalar_mul(e1c[:, 6:7], e1c[:, 6:7], -1.0))
                    P.wait("scalar", qDV, dv_nb2)
                    ep_act_sc[tt] = P.op("scalar", lambda E, tt=tt, opar=opar: E.activation(
                        outs[:, opar * D:(opar + 1) * D], accb[:, tt * D:(tt + 1) * D],
                        AF.Identity, bias=e1c[:, 6:7], scale=e1c[:, 5:6]), inc=qSC)
                    P.wait("gpsimd", qSC, ep_act_sc[tt])
                    r0 = w * WAVE + tt * 128
                    P.op("gpsimd", lambda E, r0=r0, opar=opar: E.dma_start(
                        out=out[r0:r0 + 128, :], in_=outs[:, opar * D:(opar + 1) * D]), inc=qOUT[opar])
                    outs_done[opar] = qOUT[opar].n
                ht_done_pe[par] = qPE.n

            if debug:
                P.wait("gpsimd", qDV, qDV.n)
                P.wait("gpsimd", qSC, qSC.n)
                P.wait("gpsimd", qPE, qPE.n)
                for o, i in ((dbg_wn, wnw), (dbg_hid, hid), (dbg_acc, accb),
                             (dbg_hfw, hfw), (dbg_ht, ht_b)):
                    P.op("gpsimd", lambda E, o=o, i=i: E.dma_start(out=o[:, :], in_=i[:, :]), inc=qGP)

            P.wait("gpsimd", qOUT[0], qOUT[0].n)
            P.wait("gpsimd", qOUT[1], qOUT[1].n)

            block = outer.enter_context(nc.Block())
            streams = prog.streams

            @block.tensor
            def _(eng):
                for fn in streams["tensor"]:
                    fn(eng)

            @block.vector
            def _(eng):
                for fn in streams["vector"]:
                    fn(eng)

            @block.scalar
            def _(eng):
                for fn in streams["scalar"]:
                    fn(eng)

            @block.sync
            def _(eng):
                for fn in streams["sync"]:
                    fn(eng)

            @block.gpsimd
            def _(eng):
                for fn in streams["gpsimd"]:
                    fn(eng)

    return nc


def _consts():
    sel = np.zeros((NH, 128), np.float32)
    for h in range(NH):
        sel[h, (h % 2) * 64:(h % 2) * 64 + 64] = 1.0
    mask = np.zeros((NH, NCB), np.float32)
    for h in range(NH):
        mask[h, h // 2] = 1.0
    return {
        "c_idbf": np.eye(128, dtype=BFNP),
        "c_idf": np.eye(16, dtype=np.float32),
        "c_onesrow": np.ones((1, 128), np.float32),
        "c_onescbf": np.ones((128, 1), BFNP),
        "c_ones512": np.ones((128, 512), np.float32),
        "c_sel": sel,
        "c_mask": mask,
    }


def make_in_maps(inputs):
    xs = np.ascontiguousarray(np.asarray(inputs["L0_attn_x"], dtype=np.float32)).reshape(8, TOKS, D)
    base = {
        "wq": np.asarray(inputs["L0_attn_Wq"], np.float32),
        "wv": np.asarray(inputs["L0_attn_Wv"], np.float32),
        "wg": np.asarray(inputs["L0_attn_Wg"], np.float32),
        "wo": np.asarray(inputs["L0_attn_Wo"], np.float32),
        "beta": np.asarray(inputs["L0_attn_beta"], np.float32).reshape(NH, 1),
        "gam": np.asarray(inputs["L0_attn_ln_gamma"], np.float32).reshape(D, 1),
        "bet": np.asarray(inputs["L0_attn_ln_beta"], np.float32).reshape(D, 1),
        "keys": np.asarray(inputs["L0_ffn_keys"], np.float32),
        "rwt": np.asarray(inputs["L0_ffn_rw"], np.float32),
        "rb": np.asarray(inputs["L0_ffn_rb"], np.float32).reshape(NE, 1),
        "b1": np.asarray(inputs["L0_ffn_b1"], np.float32),
        "b2": np.asarray(inputs["L0_ffn_b2"], np.float32),
        "w1": np.asarray(inputs["L0_ffn_w1"], np.float32),
        "w2": np.asarray(inputs["L0_ffn_w2"], np.float32),
    }
    base.update(_consts())
    return [dict(base, x=xs[c]) for c in range(8)]


def kernel(L0_attn_x, L0_attn_Wq, L0_attn_Wv, L0_attn_Wo, L0_attn_Wg,
           L0_attn_beta, L0_attn_ln_gamma, L0_attn_ln_beta,
           L0_ffn_keys, L0_ffn_w1, L0_ffn_b1, L0_ffn_w2, L0_ffn_b2,
           L0_ffn_rw, L0_ffn_rb):
    nc = build_program()
    in_maps = make_in_maps(dict(
        L0_attn_x=L0_attn_x, L0_attn_Wq=L0_attn_Wq, L0_attn_Wv=L0_attn_Wv,
        L0_attn_Wo=L0_attn_Wo, L0_attn_Wg=L0_attn_Wg, L0_attn_beta=L0_attn_beta,
        L0_attn_ln_gamma=L0_attn_ln_gamma, L0_attn_ln_beta=L0_attn_ln_beta,
        L0_ffn_keys=L0_ffn_keys, L0_ffn_w1=L0_ffn_w1, L0_ffn_b1=L0_ffn_b1,
        L0_ffn_w2=L0_ffn_w2, L0_ffn_b2=L0_ffn_b2, L0_ffn_rw=L0_ffn_rw,
        L0_ffn_rb=L0_ffn_rb))
    res = run_bass_kernel_spmd(nc, in_maps, core_ids=list(range(8)))
    o = np.concatenate([np.asarray(r["out"], np.float32) for r in res.results], axis=0)
    return o.reshape(16, 2048, D)
